# version 19
# baseline (speedup 1.0000x reference)
import contextlib
import numpy as np

import concourse.bass as bass
import concourse.mybir as mybir
from concourse.bass_utils import run_bass_kernel_spmd

F32 = mybir.dt.float32
F32R = mybir.dt.float32r
BF16 = mybir.dt.bfloat16
F16 = mybir.dt.float16
FP8 = mybir.dt.float8e4
AF = mybir.ActivationFunctionType
ALU = mybir.AluOpType

B, T, D = 4, 4096, 1024
TC = 2048
NT = TC // 128
NCORE = 8
CHUNK = 128
NCH = TC // CHUNK
DST = 64
GC = 64
NGC = TC // GC
NH = 4
DK = 128
DV = 256
FF = 4096
MI = 2816
LN_EPS = 1e-5
RMS_EPS = 1e-6

N_STAGE = 4

_CACHE = {}


def _host_consts():
    n = 2 * CHUNK
    k = np.arange(n // 2, dtype=np.float64)
    t = np.arange(CHUNK, dtype=np.float64)
    ang = 2.0 * np.pi * np.outer(t, k) / n
    FcT = np.cos(ang)
    FsT = -np.sin(ang)
    Fny = np.cos(np.pi * t)[:, None]
    j = np.arange(CHUNK, dtype=np.float64)
    wgt = np.full(n // 2, 2.0 / n); wgt[0] = 1.0 / n
    angi = 2.0 * np.pi * np.outer(k, j) / n
    GcT = wgt[:, None] * np.cos(angi)
    GsT = -wgt[:, None] * np.sin(angi)
    Gny = (np.cos(np.pi * j) / n)[None, :]
    tri = (np.arange(GC)[:, None] <= np.arange(GC)[None, :]).astype(np.float64)
    return {
        "c_FcT": FcT.astype(np.float32), "c_FsT": FsT.astype(np.float32),
        "c_Fny": Fny.astype(np.float32), "c_GcT": GcT.astype(np.float32),
        "c_GsT": GsT.astype(np.float32), "c_Gny": Gny.astype(np.float32),
        "c_U": (tri / 16.0).astype(np.float32),
        "c_mask": tri.astype(np.float32),
        "c_id": np.eye(128, dtype=np.float32),
        "c_ones": np.ones((1, 128), dtype=np.float32),
        "c_eps": np.repeat(np.array([[1e-5, 1e-6]], dtype=np.float32), 128, axis=0),
        "c_U2b": np.kron(np.eye(2), -tri / 16.0).astype(np.float32),
        "c_mask2": np.kron(np.eye(2), tri).astype(np.float32),
        "c_ind0": np.repeat(np.array([[-1.0 / 16.0], [0.0]], dtype=np.float32), 64, axis=0),
        "c_ind1": np.repeat(np.array([[0.0], [-1.0 / 16.0]], dtype=np.float32), 64, axis=0),
    }


class Buf:
    __slots__ = ("name", "nslot", "w", "r")

    def __init__(self, name, nslot=1, seed=()):
        self.name = name
        self.nslot = nslot
        self.w = [list(seed) for _ in range(nslot)]
        self.r = [[] for _ in range(nslot)]

    def all(self):
        return [(self, i) for i in range(self.nslot)]

    def tokens(self):
        out = []
        for i in range(self.nslot):
            out.extend(self.w[i])
            out.extend(self.r[i])
        return out


class Prog:
    ENGS = ("t", "v", "s", "g", "y")

    def __init__(self, nc):
        self.nc = nc
        self.ops = {e: [] for e in self.ENGS}
        self.counts = {}
        self.handles = {}
        self.last_wait = {e: {} for e in self.ENGS}
        self._es = contextlib.ExitStack()
        for e in self.ENGS:
            self._sem("c" + e)
        self.dma_rr = {}
        for q in ("dy", "ds", "dg"):
            self.dma_rr[q] = 0
            for i in range(4):
                self._sem(q + str(i))
        self._sem("cc")

    def _sem(self, name):
        self.counts[name] = 0
        self.handles[name] = self._es.enter_context(self.nc.semaphore(name))

    @staticmethod
    def _norm(lst):
        out = []
        for x in lst or []:
            if x is None:
                continue
            if isinstance(x, Buf):
                out.append((x, 0))
            elif isinstance(x, tuple) and isinstance(x[0], Buf):
                out.append(x)
            elif isinstance(x, list):
                out.extend(Prog._norm(x))
            else:
                raise TypeError(f"bad dep {x}")
        return out

    def emit(self, eng, fn, reads=(), writes=(), extra_waits=(), dma=False,
             is_collective=False):
        reads = self._norm(reads)
        writes = self._norm(writes)
        waits = {}

        def add_tok(tok):
            if tok is None:
                return
            sn, v = tok
            if waits.get(sn, 0) < v:
                waits[sn] = v

        for (b, i) in reads:
            for tok in b.w[i]:
                add_tok(tok)
        for (b, i) in writes:
            for tok in b.w[i]:
                add_tok(tok)
            for tok in b.r[i]:
                add_tok(tok)
        for tok in extra_waits or []:
            add_tok(tok)

        if is_collective:
            semn, incby = "cc", 1
        elif dma:
            q = {"y": "dy", "s": "ds", "g": "dg"}[eng]
            ri = self.dma_rr[q]
            self.dma_rr[q] = (ri + 1) % 4
            semn = q + str(ri)
            incby = 16
            add_tok((semn, self.counts[semn]))
        else:
            semn, incby = "c" + eng, 1
        self.counts[semn] += incby
        tok = (semn, self.counts[semn])

        lw = self.last_wait[eng]
        eff = []
        for sn, v in waits.items():
            if lw.get(sn, 0) >= v:
                continue
            lw[sn] = v
            eff.append((sn, v))
        handles = self.handles

        def run(engine, eff=eff, fn=fn, semn=semn, incby=incby):
            for (sn, v) in eff:
                engine.wait_ge(handles[sn], v)
            instr = fn(engine)
            instr.then_inc(handles[semn], incby)
            return instr

        self.ops[eng].append(run)
        for (b, i) in reads:
            b.r[i].append(tok)
        for (b, i) in writes:
            b.w[i] = [tok]
            b.r[i] = []
        return tok

    def wait_only(self, eng, deps=(), extra_waits=()):
        return self.emit(eng, lambda e: e.nop(), reads=deps, extra_waits=extra_waits)

    def replay(self):
        block = self._es.enter_context(self.nc.Block())
        ops = self.ops

        @block.tensor
        def _(e):
            for f in ops["t"]: f(e)

        @block.vector
        def _(e):
            for f in ops["v"]: f(e)

        @block.scalar
        def _(e):
            for f in ops["s"]: f(e)

        @block.gpsimd
        def _(e):
            for f in ops["g"]: f(e)

        @block.sync
        def _(e):
            for f in ops["y"]: f(e)


def build_nc(n_stage):
    nc = bass.Bass("TRN2", target_bir_lowering=False)
    consts = _host_consts()

    dr = {}
    def din(name, shape):
        dr[name] = nc.dram_tensor(name, list(shape), F32, kind="ExternalInput").ap()
        return dr[name]

    din("x0", (TC, D))
    din("W_inT", (D, D)); din("b_in", (D,))
    din("gamma", (D,)); din("beta", (D,))
    din("A_re", (66,)); din("A_im", (66,))
    din("B_re", (66,)); din("B_im", (66,))
    din("CreT", (66, D)); din("CimT", (66, D))
    din("w1T", (D, FF)); din("b1", (FF,))
    din("w2T", (FF, D)); din("b2", (D,))
    din("attn_w", (D,))
    din("Wq", (D, 512)); din("Wk", (D, 512))
    din("Wv", (D, D)); din("Wg", (D, D))
    din("Wgk1", (D, 16)); din("Wgk2", (16, 512))
    din("bgk2", (512,)); din("gnw", (DV,))
    din("Wo", (D, D)); din("mlp_w", (D,))
    din("Wgate", (D, MI)); din("Wup", (D, MI))
    din("Wdown", (MI, D))
    din("selc", (128, 8))
    for cname, arr in consts.items():
        din(cname, arr.shape)

    out = nc.dram_tensor("out", [TC, D], F32, kind="ExternalOutput").ap()
    outT = out.rearrange("(tt p) d -> p tt d", p=128)

    ag1_in = nc.dram_tensor("ag1_in", [132, D], F32).ap()
    ag1_out = nc.dram_tensor("ag1_out", [NCORE * 132, D], F32).ap()
    ag2_in = nc.dram_tensor("ag2_in", [128, NH * DV], F32).ap()
    ag2_out = nc.dram_tensor("ag2_out", [NCORE * 128, NH * DV], F32).ap()
    spill = nc.dram_tensor("x2b_spill", [TC, D], BF16).ap()
    w1sp = nc.dram_tensor("w1_spill", [128, 8, FF], BF16).ap()
    w2sp = nc.dram_tensor("w2_spill", [128, 32, D], FP8).ap()
    Bw1sp = Buf("w1sp", 1)
    Bw2sp = Buf("w2sp", 1)

    p = Prog(nc)
    es = p._es

    live = {}
    freed_tokens = []
    alloc_seq = [0]

    def sb(name, shape, dtype, nslot=1):
        cm = nc.sbuf_tensor("sb_" + name, list(shape), dtype)
        t = cm.__enter__()
        b = Buf(name, nslot, seed=tuple(freed_tokens))
        alloc_seq[0] += 1
        live[name] = (alloc_seq[0], cm, b)
        return t, b

    def sbfree(*names):
        for name in sorted(names, key=lambda n: -live[n][0]):
            _, cm, b = live.pop(name)
            freed_tokens.extend(b.tokens())
            cm.__exit__(None, None, None)
        best = {}
        for (sn, v) in freed_tokens:
            if best.get(sn, 0) < v:
                best[sn] = v
        freed_tokens[:] = list(best.items())

    def psum(name, shape, dtype=F32):
        t = es.enter_context(nc.psum_tensor("ps_" + name, list(shape), dtype))
        b = Buf(name, (shape[-1] * 4 + 2047) // 2048)
        return t, b

    def pb_(B, sl):
        return [(B, k) for k in range(sl.start // 512, (sl.stop + 511) // 512)]

    stream, Bstream = sb("stream", (128, NT, D), F32, nslot=NT)

    c_FcT, BcFcT = sb("c_FcT", (128, 128), F32R)
    c_FsT, BcFsT = sb("c_FsT", (128, 128), F32R)
    c_Fny, BcFny = sb("c_Fny", (128, 1), F32R)
    c_GcT, BcGcT = sb("c_GcT", (128, 128), F32R)
    c_GsT, BcGsT = sb("c_GsT", (128, 128), F32R)
    c_Gny, BcGny = sb("c_Gny", (1, 128), F32R)
    c_U, BcU = sb("c_U", (64, 64), F32R)
    c_mask, Bcmask = sb("c_mask", (64, 64), F32)
    c_id, Bcid = sb("c_id", (128, 128), F32)
    c_idb, Bcidb = sb("c_idb", (128, 128), BF16)
    c_ones, Bcones = sb("c_ones", (1, 128), F32R)
    c_eps, Bceps = sb("c_eps", (128, 2), F32)
    nc.const_aps.register(F32, LN_EPS, c_eps[:, 0:1])
    nc.const_aps.register(F32, RMS_EPS, c_eps[:, 1:2])

    bnscr, Bbn = sb("bnscr", (128, 2, 12), F32, nslot=2)
    mvt, Bmvt = sb("mvt", (128, 2, 2), F32, nslot=2)
    stat, Bstat = sb("stat", (128, 8 * NT), F32, nslot=NT)
    xh, Bxh = sb("xh", (128, 2, D), BF16, nslot=2)
    xhT, BxhT = sb("xhT", (128, 2, 8, 128), BF16, nslot=2)
    selcs, Bsel = sb("selcs", (128, 8), F32)

    pX, BpX = psum("pX", (128, 2048))
    pU, BpU = psum("pU", (128, 1024))
    pY, BpY = psum("pY", (128, 1024))
    pTb = pU[:].bitcast(BF16)

    def dma(queue, out_ap, in_ap, reads=(), writes=()):
        return p.emit(queue, lambda e: e.dma_start(out=out_ap, in_=in_ap),
                      reads=reads, writes=writes, dma=True)

    dma("y", stream[:], dr["x0"].rearrange("(tt p) d -> p tt d", p=128),
        writes=Bstream.all())
    for nm, tile, bb in (("c_FcT", c_FcT, BcFcT), ("c_FsT", c_FsT, BcFsT),
                         ("c_Fny", c_Fny, BcFny), ("c_GcT", c_GcT, BcGcT),
                         ("c_GsT", c_GsT, BcGsT), ("c_Gny", c_Gny, BcGny),
                         ("c_U", c_U, BcU), ("c_ones", c_ones, Bcones)):
        dma("g", tile[:], dr[nm], writes=[bb])
    for nm, tile, bb in (("c_mask", c_mask, Bcmask), ("c_id", c_id, Bcid),
                         ("c_eps", c_eps, Bceps)):
        dma("s", tile[:], dr[nm], writes=[bb])
    dma("g", c_idb[:], dr["c_id"], writes=[Bcidb])
    dma("s", selcs[:], dr["selc"], writes=[Bsel])

    def ln_stats_pair(i, col0, src_bufs, src0, src1, eps=LN_EPS):
        sl2 = i % 2
        bn = bnscr[:, sl2, :]
        mv = mvt[:, sl2, :]
        p.emit("v", lambda e, bn=bn: e.bn_stats(bn[:, 0:6], src0), reads=src_bufs,
               writes=[(Bbn, sl2)])
        p.emit("v", lambda e, bn=bn: e.bn_stats(bn[:, 6:12], src1), reads=src_bufs,
               writes=[(Bbn, sl2)])
        p.emit("v", lambda e, bn=bn, mv=mv: e.bn_aggr(mv, bn),
               reads=[(Bbn, sl2)], writes=[(Bmvt, sl2)])
        c = 8 * i + col0
        p.emit("s", lambda e, c=c, eps=eps, mv=mv: e.activation(
            out=stat[:, c:c + 1], in_=mv[:, 1:2], func=AF.Ln, bias=eps, scale=1.0),
            reads=[(Bmvt, sl2), Bceps], writes=[(Bstat, i)])
        p.emit("s", lambda e, c=c: e.activation(
            out=stat[:, c + 1:c + 2], in_=stat[:, c:c + 1], func=AF.Exp,
            bias=0.0, scale=-0.5),
            reads=[(Bstat, i)], writes=[(Bstat, i)])
        p.emit("v", lambda e, c=c, mv=mv: e.tensor_scalar(
            out=stat[:, c + 2:c + 3], in0=mv[:, 0:1],
            scalar1=stat[:, c + 1:c + 2], scalar2=-1.0, op0=ALU.mult, op1=ALU.mult),
            reads=[(Bmvt, sl2), (Bstat, i)], writes=[(Bstat, i)])

    def tp128(src_ap, src_bufs, dst_ap, dst_bufs, k, dtype="b"):
        bank = k % 2
        if dtype == "b":
            pdst = pTb[:, 1024 * bank:1024 * bank + 128]
        else:
            pdst = pU[:, 512 * bank:512 * bank + 128]
        ident = c_idb if dtype == "b" else c_id
        identb = Bcidb if dtype == "b" else Bcid
        p.emit("t", lambda e: e.matmul(pdst, src_ap, ident[:],
                                       start=True, stop=True, is_transpose=True),
               reads=list(src_bufs) + [identb], writes=[(BpU, bank)])
        p.emit("v", lambda e: e.tensor_copy(dst_ap, pdst),
               reads=[(BpU, bank)], writes=list(dst_bufs))

    def norm_transpose(src_ap, i, col0, src_bufs, slot):
        c = 8 * i + col0
        p.emit("s", lambda e, c=c, src_ap=src_ap, slot=slot: e.activation(
            out=xh[:, slot, :], in_=src_ap, func=AF.Identity,
            bias=stat[:, c + 2:c + 3], scale=stat[:, c + 1:c + 2]),
            reads=list(src_bufs) + [(Bstat, i)], writes=[(Bxh, slot)])
        for kdc in range(8):
            tp128(xh[:, slot, 128 * kdc:128 * (kdc + 1)], [(Bxh, slot)],
                  xhT[:, slot, kdc, :], [(BxhT, slot)], kdc)

    def ts(eng, out_ap, in_ap, s1, s2, op0, op1, rb, wb):
        p.emit(eng, lambda e: e.tensor_scalar(out=out_ap, in0=in_ap, scalar1=s1,
                                              scalar2=s2, op0=op0, op1=op1),
               reads=list(rb), writes=list(wb))

    def stt(eng, out_ap, in0_ap, scalar, in1_ap, op0, op1, rb, wb):
        p.emit(eng, lambda e: e.scalar_tensor_tensor(out=out_ap, in0=in0_ap,
                                                     scalar=scalar, in1=in1_ap,
                                                     op0=op0, op1=op1),
               reads=list(rb), writes=list(wb))

    def tt(eng, out_ap, in0_ap, in1_ap, op, rb, wb):
        p.emit(eng, lambda e: e.tensor_tensor(out=out_ap, in0=in0_ap, in1=in1_ap,
                                              op=op),
               reads=list(rb), writes=list(wb))

    evs, Bevs = sb("evs", (128, D), F32)
    tscr, Btscr = sb("tscr", (128, D), F32)
    tscr2, Btscr2 = sb("tscr2", (128, D), F32)
    aghold, Bagh = sb("aghold", (128, D), F32)
    abuf, Babuf = sb("abuf", (128, NCH, D), F32R, nslot=NCH)
    wWinT, BwWinT = sb("wWinT", (128, 8, D), BF16)
    brow, Bbrow = sb("brow", (1, D), F32R)
    gamma_b, Bgamma = sb("gamma_b", (128, D), F32)
    beta_b, Bbeta = sb("beta_b", (128, D), F32)
    growg, Bgrowg = sb("growg", (1, D), F32R)
    growb, Bgrowb = sb("growb", (1, D), F32R)

    dma("g", wWinT[:], dr["W_inT"].rearrange("(c p) d -> p c d", p=128),
        writes=[BwWinT])
    dma("g", brow[:], dr["b_in"][None, :], writes=[Bbrow])
    dma("g", growg[:], dr["gamma"][None, :], writes=[Bgrowg])
    dma("g", growb[:], dr["beta"][None, :], writes=[Bgrowb])

    for srcrow, bsrc, dst, bb in ((growg, Bgrowg, gamma_b, Bgamma),
                                  (growb, Bgrowb, beta_b, Bbeta)):
        for nh2 in range(2):
            p.emit("t", lambda e, srcrow=srcrow, nh2=nh2: e.matmul(
                pY[:, 0:512], c_ones[:], srcrow[:, 512 * nh2:512 * (nh2 + 1)],
                start=True, stop=True),
                reads=[bsrc, Bcones], writes=[(BpY, 0)])
            p.emit("v", lambda e, dst=dst, nh2=nh2: e.tensor_copy(
                dst[:, 512 * nh2:512 * (nh2 + 1)], pY[:, 0:512]),
                reads=[(BpY, 0)], writes=[(bb, 0)])

    for i in range(NT):
        xt = stream[:, i, :]
        ln_stats_pair(i, 0, [(Bstream, i)], xt[:, 0:512], xt[:, 512:1024])

    def a_prep(i):
        norm_transpose(stream[:, i, :], i, 0, [(Bstream, i)], i % 2)

    def a_mm(i):
        slot = i % 2
        for nh2 in range(2):
            sl = slice(512 * nh2, 512 * (nh2 + 1))
            for kdc in range(8):
                p.emit("t", lambda e, kdc=kdc, sl=sl, slot=slot: e.matmul(
                    pY[:, sl], xhT[:, slot, kdc, :], wWinT[:, kdc, sl],
                    start=(kdc == 0), stop=False),
                    reads=[(BxhT, slot), BwWinT],
                    writes=[(BpY, nh2)] if kdc == 0 else [])
            p.emit("t", lambda e, sl=sl, nh2=nh2: e.matmul(
                pY[:, sl], c_ones[:], brow[:, sl], start=False, stop=True),
                reads=[Bcones, Bbrow], writes=[(BpY, nh2)])
        ln_stats_pair(i, 3, [(BpY, 0), (BpY, 1)], pY[:, 0:512], pY[:, 512:1024])

    def a_evac(i):
        c = 8 * i + 3
        for nh2 in range(2):
            sl = slice(512 * nh2, 512 * (nh2 + 1))
            p.emit("s", lambda e, c=c, sl=sl, nh2=nh2: e.activation(
                out=evs[:, sl], in_=pY[:, sl], func=AF.Identity,
                bias=stat[:, c + 2:c + 3], scale=stat[:, c + 1:c + 2]),
                reads=[(BpY, nh2), (Bstat, i)], writes=[Bevs])
        tt("v", evs[:], evs[:], gamma_b[:], ALU.mult, [Bevs, Bgamma], [Bevs])
        p.emit("g", lambda e, i=i: e.tensor_tensor(
            out=abuf[:, i, :], in0=evs[:], in1=beta_b[:], op=ALU.add),
            reads=[Bevs, Bbeta], writes=[(Babuf, i)])

    stages = [a_prep, a_mm, a_evac]
    for step in range(NT + len(stages) - 1):
        for si in range(len(stages) - 1, -1, -1):
            i = step - si
            if 0 <= i < NT:
                stages[si](i)

    sbfree("wWinT", "brow", "gamma_b", "beta_b", "growg", "growb")

    if n_stage == 0:
        for i in range(NT):
            dma("y", outT[:, i, :], abuf[:, i, :].bitcast(F32), reads=[(Babuf, i)])
        p.wait_only("y", extra_waits=[("dy" + str(i), p.counts["dy" + str(i)])
                                      for i in range(4)])
        p.replay()
        return nc

    NS = 66
    Kfre, BKfre = sb("Kfre", (128, D), F32)
    Kfim, BKfim = sb("Kfim", (128, D), F32)
    Kfny, BKfny = sb("Kfny", (1, D), F32)
    PTre, BPTre = sb("PTre", (128, NS), F32R)
    PTim, BPTim = sb("PTim", (128, NS), F32R)
    QTre, BQTre = sb("QTre", (NS, 128), F32R)
    QTimN, BQTimN = sb("QTimN", (NS, 128), F32R)
    accol, Baccol = sb("accol", (NS, 3), F32)
    wcre, Bwcre = sb("wcre", (NS, D), F32)
    wcim, Bwcim = sb("wcim", (NS, D), F32)
    Sre, BSre = sb("Sre", (NS, D), F32)
    Sim, BSim = sb("Sim", (NS, D), F32)
    Gre, BGre = sb("Gre", (NS, D), F32R)
    Gim, BGim = sb("Gim", (NS, D), F32R)
    PreS, BPreS = sb("PreS", (128, D), F32R)
    PimS, BPimS = sb("PimS", (128, D), F32R)
    pnyr, Bpnyr = sb("pnyr", (1, D), F32R)
    acol, Bacol = sb("acol", (NS, 4), F32)
    mcol, Bmcol = sb("mcol", (NS, 8), F32)
    negb, Bnegb = sb("negb", (NS, 2), F32)
    pwre, Bpwre = Kfre[0:NS, 0:256], BKfre
    pwim, Bpwim = Kfim[0:NS, 0:256], BKfim
    pvre, Bpvre = Kfre[0:NS, 256:384], BKfre
    pvim, Bpvim = Kfim[0:NS, 256:384], BKfim
    pbre, Bpbre = sb("pbre", (NS, 128), F32R)
    pbimN, BpbimN = sb("pbimN", (NS, 128), F32R)
    CreTr, BCreTr = Gre[:], BGre
    CimTr, BCimTr = Gim[:], BGim
    Krs, BKrs = PreS[:], BPreS

    dma("s", acol[:, 0:1], dr["A_re"][:, None], writes=[Bacol])
    dma("s", acol[:, 1:2], dr["A_im"][:, None], writes=[Bacol])
    dma("s", acol[:, 2:3], dr["B_re"][:, None], writes=[Bacol])
    dma("s", acol[:, 3:4], dr["B_im"][:, None], writes=[Bacol])
    dma("g", CreTr, dr["CreT"], writes=[BCreTr])
    dma("g", CimTr, dr["CimT"], writes=[BCimTr])

    p.emit("g", lambda e: e.memset(pwre[:, 0:1], 1.0), writes=[Bpwre])
    p.emit("g", lambda e: e.memset(pwim[:, 0:1], 0.0), writes=[Bpwim])
    p.emit("g", lambda e: e.memset(pvre[:, 127:128], 1.0), writes=[Bpvre])
    p.emit("g", lambda e: e.memset(pvim[:, 127:128], 0.0), writes=[Bpvim])
    p.emit("v", lambda e: e.tensor_copy(mcol[:, 0:2], acol[:, 0:2]),
           reads=[Bacol], writes=[Bmcol])
    ts("v", mcol[:, 2:3], acol[:, 1:2], -1.0, None, ALU.mult, ALU.bypass,
       [Bacol], [Bmcol])
    cur = 0
    for k in range(8):
        w = 1 << k
        mre = mcol[:, 3 * cur + 0:3 * cur + 1]
        mim = mcol[:, 3 * cur + 1:3 * cur + 2]
        mnim = mcol[:, 3 * cur + 2:3 * cur + 3]
        ts("v", tscr[0:NS, 0:w], pwre[:, 0:w], mre, None, ALU.mult, ALU.bypass,
           [Bpwre, Bmcol], [Btscr])
        stt("v", pwre[:, w:2 * w], pwim[:, 0:w], mnim, tscr[0:NS, 0:w],
            ALU.mult, ALU.add, [Bpwim, Bmcol, Btscr], [Bpwre])
        ts("v", tscr[0:NS, 0:w], pwre[:, 0:w], mim, None, ALU.mult, ALU.bypass,
           [Bpwre, Bmcol], [Btscr])
        stt("v", pwim[:, w:2 * w], pwim[:, 0:w], mre, tscr[0:NS, 0:w],
            ALU.mult, ALU.add, [Bpwim, Bmcol, Btscr], [Bpwim])
        if w < 128:
            lo, hi = 128 - 2 * w, 128 - w
            ts("v", tscr[0:NS, 0:w], pvre[:, hi:128], mre, None, ALU.mult,
               ALU.bypass, [Bpvre, Bmcol], [Btscr])
            stt("v", pvre[:, lo:hi], pvim[:, hi:128], mnim, tscr[0:NS, 0:w],
                ALU.mult, ALU.add, [Bpvim, Bmcol, Btscr], [Bpvre])
            ts("v", tscr[0:NS, 0:w], pvre[:, hi:128], mim, None, ALU.mult,
               ALU.bypass, [Bpvre, Bmcol], [Btscr])
            stt("v", pvim[:, lo:hi], pvim[:, hi:128], mre, tscr[0:NS, 0:w],
                ALU.mult, ALU.add, [Bpvim, Bmcol, Btscr], [Bpvim])
        if k < 7:
            nxt = 1 - cur
            nre = mcol[:, 3 * nxt + 0:3 * nxt + 1]
            nim = mcol[:, 3 * nxt + 1:3 * nxt + 2]
            nnim = mcol[:, 3 * nxt + 2:3 * nxt + 3]
            ts("v", mcol[:, 6:7], mre, mre, None, ALU.mult, ALU.bypass,
               [Bmcol], [Bmcol])
            stt("v", nre, mim, mnim, mcol[:, 6:7], ALU.mult, ALU.add,
                [Bmcol], [Bmcol])
            ts("v", nim, mre, mim, 2.0, ALU.mult, ALU.mult, [Bmcol], [Bmcol])
            ts("v", nnim, nim, -1.0, None, ALU.mult, ALU.bypass, [Bmcol], [Bmcol])
            cur = nxt

    ts("v", negb[:, 0:1], acol[:, 3:4], -1.0, None, ALU.mult, ALU.bypass,
       [Bacol], [Bnegb])
    ts("v", negb[:, 1:2], acol[:, 2:3], -1.0, None, ALU.mult, ALU.bypass,
       [Bacol], [Bnegb])
    ts("v", tscr[0:NS, 0:128], pwre[:, 0:128], acol[:, 2:3], None, ALU.mult,
       ALU.bypass, [Bpwre, Bacol], [Btscr])
    stt("v", pbre[:], pwim[:, 0:128], negb[:, 0:1], tscr[0:NS, 0:128],
        ALU.mult, ALU.add, [Bpwim, Bnegb, Btscr], [Bpbre])
    ts("v", tscr[0:NS, 0:128], pwre[:, 0:128], negb[:, 0:1], None, ALU.mult,
       ALU.bypass, [Bpwre, Bnegb], [Btscr])
    stt("v", pbimN[:], pwim[:, 0:128], negb[:, 1:2], tscr[0:NS, 0:128],
        ALU.mult, ALU.add, [Bpwim, Bnegb, Btscr], [BpbimN])

    ts("v", tscr[0:NS, :], CreTr.bitcast(F32), acol[:, 2:3], None, ALU.mult,
       ALU.bypass, [BCreTr, Bacol], [Btscr])
    stt("v", wcre[:], CimTr.bitcast(F32), negb[:, 0:1], tscr[0:NS, :],
        ALU.mult, ALU.add, [BCimTr, Bnegb, Btscr], [Bwcre])
    ts("v", tscr[0:NS, :], CreTr.bitcast(F32), acol[:, 3:4], None, ALU.mult,
       ALU.bypass, [BCreTr, Bacol], [Btscr])
    stt("v", wcim[:], CimTr.bitcast(F32), acol[:, 2:3], tscr[0:NS, :],
        ALU.mult, ALU.add, [BCimTr, Bacol, Btscr], [Bwcim])

    for src, sbf, dst, dbf in ((pvre, Bpvre, PTre, BPTre), (pvim, Bpvim, PTim, BPTim)):
        p.emit("t", lambda e, src=src: e.matmul(
            pU[:, 0:NS], src[:], c_id[0:NS, 0:NS],
            start=True, stop=True, is_transpose=True),
            reads=[sbf, Bcid], writes=[(BpU, 0)])
        p.emit("s", lambda e, dst=dst: e.activation(
            out=dst[:], in_=pU[:, 0:NS], func=AF.Identity,
            bias=0.0, scale=1.0),
            reads=[(BpU, 0)], writes=[dbf])
    p.emit("s", lambda e: e.activation(out=QTre[:], in_=pwre[:, 1:129],
                                       func=AF.Identity, bias=0.0, scale=1.0),
           reads=[Bpwre], writes=[BQTre])
    ts("v", QTimN[:], pwim[:, 1:129], -1.0, None, ALU.mult, ALU.bypass,
       [Bpwim], [BQTimN])
    p.emit("v", lambda e: e.tensor_copy(accol[:, 0:1], pwre[:, 128:129]),
           reads=[Bpwre], writes=[Baccol])
    p.emit("v", lambda e: e.tensor_copy(accol[:, 1:2], pwim[:, 128:129]),
           reads=[Bpwim], writes=[Baccol])
    ts("v", accol[:, 2:3], pwim[:, 128:129], -1.0, None, ALU.mult, ALU.bypass,
       [Bpwim], [Baccol])


    for nh2 in range(2):
        sl = slice(512 * nh2, 512 * (nh2 + 1))
        p.emit("t", lambda e, sl=sl: e.matmul(pY[:, sl], pbre[:], CreTr[:, sl],
                                              start=True, stop=False),
               reads=[Bpbre, BCreTr], writes=[(BpY, nh2)])
        p.emit("t", lambda e, sl=sl: e.matmul(pY[:, sl], pbimN[:], CimTr[:, sl],
                                              start=False, stop=True),
               reads=[BpbimN, BCimTr], writes=[(BpY, nh2)])
    p.emit("s", lambda e: e.activation(out=Krs[:], in_=pY[:], func=AF.Identity,
                                       bias=0.0, scale=1.0),
           reads=[(BpY, 0), (BpY, 1)], writes=[BKrs])
    for comp, lhs, lb, dstb, dst in ((0, c_FcT, BcFcT, BKfre, Kfre),
                                     (1, c_FsT, BcFsT, BKfim, Kfim)):
        for nh2 in range(2):
            sl = slice(512 * nh2, 512 * (nh2 + 1))
            po = slice(1024 * comp + 512 * nh2, 1024 * comp + 512 * (nh2 + 1))
            p.emit("t", lambda e, lhs=lhs, sl=sl, po=po: e.matmul(
                pX[:, po], lhs[:], Krs[:, sl], start=True, stop=True),
                reads=[lb, BKrs], writes=pb_(BpX, po))
        p.emit("s", lambda e, comp=comp, dst=dst: e.activation(
            out=dst[:], in_=pX[:, 1024 * comp:1024 * (comp + 1)],
            func=AF.Identity, bias=0.0, scale=1.0),
            reads=[(BpX, 2 * comp), (BpX, 2 * comp + 1)], writes=[dstb])
    for nh2 in range(2):
        sl = slice(512 * nh2, 512 * (nh2 + 1))
        p.emit("t", lambda e, sl=sl, nh2=nh2: e.matmul(pU[0:1, sl], c_Fny[:],
                                              Krs[:, sl], start=True, stop=True),
               reads=[BcFny, BKrs], writes=[(BpU, nh2)])
    p.emit("v", lambda e: e.tensor_copy(Kfny[:], pU[0:1, :]),
           reads=[(BpU, 0), (BpU, 1)], writes=[BKfny])
    sbfree("acol", "mcol", "negb", "pbre", "pbimN")

    p.emit("g", lambda e: e.memset(Sre[:], 0.0), writes=[BSre])
    p.emit("g", lambda e: e.memset(Sim[:], 0.0), writes=[BSim])

    def U_mms(i):
        for comp, lhs, lb in ((0, PTre, BPTre), (1, PTim, BPTim)):
            for nh2 in range(2):
                sl = slice(512 * nh2, 512 * (nh2 + 1))
                po = slice(1024 * comp + 512 * nh2, 1024 * comp + 512 * (nh2 + 1))
                p.emit("t", lambda e, lhs=lhs, sl=sl, po=po, i=i: e.matmul(
                    pX[0:NS, po], lhs[:], abuf[:, i, sl], start=True, stop=True),
                    reads=[lb, (Babuf, i)], writes=pb_(BpX, po))

    def S_update():
        ts("v", tscr[0:NS, :], Sre[:], accol[:, 0:1], None, ALU.mult, ALU.bypass,
           [BSre, Baccol], [Btscr])
        stt("v", tscr[0:NS, :], Sim[:], accol[:, 2:3], tscr[0:NS, :],
            ALU.mult, ALU.add, [BSim, Baccol, Btscr], [Btscr])
        ts("v", tscr2[0:NS, :], Sim[:], accol[:, 0:1], None, ALU.mult, ALU.bypass,
           [BSim, Baccol], [Btscr2])
        stt("v", tscr2[0:NS, :], Sre[:], accol[:, 1:2], tscr2[0:NS, :],
            ALU.mult, ALU.add, [BSre, Baccol, Btscr2], [Btscr2])
        tt("v", Sre[:], tscr[0:NS, :], pX[0:NS, 0:1024], ALU.add,
           [Btscr, (BpX, 0), (BpX, 1)], [BSre])
        tt("v", Sim[:], tscr2[0:NS, :], pX[0:NS, 1024:2048], ALU.add,
           [Btscr2, (BpX, 2), (BpX, 3)], [BSim])

    for i in range(NCH):
        U_mms(i)
        S_update()

    dma("g", ag1_in[0:NS, :], Sre[:], reads=[BSre])
    dma("g", ag1_in[NS:2 * NS, :], Sim[:], reads=[BSim])
    cc_tok = p.emit("g", lambda e: e.collective_compute(
        "AllGather", ALU.bypass, ins=[ag1_in.opt()], outs=[ag1_out.opt()],
        replica_groups=[list(range(NCORE))]),
        extra_waits=[("dg" + str(i), p.counts["dg" + str(i)]) for i in range(4)],
        is_collective=True)
    p.emit("g", lambda e: e.memset(Sre[:], 0.0), writes=[BSre])
    p.emit("g", lambda e: e.memset(Sim[:], 0.0), writes=[BSim])
    for r in range(NCORE):
        p.emit("g", lambda e, r=r: e.dma_start(
            out=aghold[0:NS, :], in_=ag1_out[2 * NS * r:2 * NS * r + NS, :]),
            writes=[Bagh], extra_waits=[cc_tok], dma=True)
        p.emit("g", lambda e, r=r: e.dma_start(
            out=evs[0:NS, :], in_=ag1_out[2 * NS * r + NS:2 * NS * (r + 1), :]),
            writes=[Bevs], extra_waits=[cc_tok], dma=True)
        stt("v", Sre[:], aghold[0:NS, :], selcs[0:NS, r:r + 1], Sre[:],
            ALU.mult, ALU.add, [Bagh, Bsel, BSre], [BSre])
        stt("v", Sim[:], evs[0:NS, :], selcs[0:NS, r:r + 1], Sim[:],
            ALU.mult, ALU.add, [Bevs, Bsel, BSim], [BSim])

    xnyr = aghold[0:1, :]
    Bxnyr = Bagh

    for i in range(NCH):
        tt("g", tscr[0:NS, :], wcre[:], Sre[:], ALU.mult, [Bwcre, BSre], [Btscr])
        tt("g", tscr2[0:NS, :], wcim[:], Sim[:], ALU.mult, [Bwcim, BSim], [Btscr2])
        tt("g", Gre[:], tscr[0:NS, :], tscr2[0:NS, :], ALU.subtract,
           [Btscr, Btscr2], [BGre])
        tt("g", tscr[0:NS, :], wcre[:], Sim[:], ALU.mult, [Bwcre, BSim], [Btscr])
        tt("g", tscr2[0:NS, :], wcim[:], Sre[:], ALU.mult, [Bwcim, BSre], [Btscr2])
        tt("g", Gim[:], tscr[0:NS, :], tscr2[0:NS, :], ALU.add,
           [Btscr, Btscr2], [BGim])
        for nh2 in range(2):
            sl = slice(512 * nh2, 512 * (nh2 + 1))
            p.emit("t", lambda e, sl=sl, i=i, nh2=nh2: e.matmul(
                pU[0:1, sl], c_Fny[:], abuf[:, i, sl],
                start=True, stop=True),
                reads=[BcFny, (Babuf, i)], writes=[(BpU, nh2)])
        p.emit("v", lambda e: e.tensor_copy(xnyr, pU[0:1, :]),
               reads=[(BpU, 0), (BpU, 1)], writes=[Bxnyr])
        for comp, lhs, lb in ((0, c_FcT, BcFcT), (1, c_FsT, BcFsT)):
            for nh2 in range(2):
                sl = slice(512 * nh2, 512 * (nh2 + 1))
                po = slice(1024 * comp + 512 * nh2, 1024 * comp + 512 * (nh2 + 1))
                p.emit("t", lambda e, lhs=lhs, sl=sl, po=po, i=i: e.matmul(
                    pX[:, po], lhs[:], abuf[:, i, sl], start=True, stop=True),
                    reads=[lb, (Babuf, i)], writes=pb_(BpX, po))
        tt("v", tscr[:], pX[:, 0:1024], Kfre[:], ALU.mult,
           [(BpX, 0), (BpX, 1), BKfre], [Btscr])
        tt("v", tscr2[:], pX[:, 1024:2048], Kfim[:], ALU.mult,
           [(BpX, 2), (BpX, 3), BKfim], [Btscr2])
        tt("g", PreS[:], tscr[:], tscr2[:], ALU.subtract, [Btscr, Btscr2], [BPreS])
        tt("v", tscr[:], pX[:, 0:1024], Kfim[:], ALU.mult,
           [(BpX, 0), (BpX, 1), BKfim], [Btscr])
        tt("v", tscr2[:], pX[:, 1024:2048], Kfre[:], ALU.mult,
           [(BpX, 2), (BpX, 3), BKfre], [Btscr2])
        tt("g", PimS[:], tscr[:], tscr2[:], ALU.add, [Btscr, Btscr2], [BPimS])
        tt("v", pnyr[:], xnyr, Kfny[:], ALU.mult, [Bxnyr, BKfny], [Bpnyr])
        for nh2 in range(2):
            sl = slice(512 * nh2, 512 * (nh2 + 1))
            p.emit("t", lambda e, sl=sl: e.matmul(pY[:, sl], c_GcT[:], PreS[:, sl],
                                                  start=True, stop=False),
                   reads=[BcGcT, BPreS], writes=[(BpY, nh2)])
            p.emit("t", lambda e, sl=sl: e.matmul(pY[:, sl], c_GsT[:], PimS[:, sl],
                                                  start=False, stop=False),
                   reads=[BcGsT, BPimS])
            p.emit("t", lambda e, sl=sl: e.matmul(pY[:, sl], QTre[:], Gre[:, sl],
                                                  start=False, stop=False),
                   reads=[BQTre, BGre])
            p.emit("t", lambda e, sl=sl: e.matmul(pY[:, sl], QTimN[:], Gim[:, sl],
                                                  start=False, stop=False),
                   reads=[BQTimN, BGim])
            p.emit("t", lambda e, sl=sl, nh2=nh2: e.matmul(
                pY[:, sl], c_Gny[:], pnyr[:, sl], start=False, stop=True),
                reads=[BcGny, Bpnyr], writes=[(BpY, nh2)])
        U_mms(i)
        for nh2 in range(2):
            sl = slice(512 * nh2, 512 * (nh2 + 1))
            stt("v", evs[:, sl], pY[:, sl], 1.0, abuf[:, i, sl].bitcast(F32),
                ALU.mult, ALU.add, [(BpY, nh2), (Babuf, i)], [Bevs])
        p.emit("g", lambda e, i=i: e.tensor_tensor(
            out=stream[:, i, :], in0=stream[:, i, :], in1=evs[:], op=ALU.add),
            reads=[Bevs, (Bstream, i)], writes=[(Bstream, i)])
        S_update()

    sbfree("pnyr", "Kfre", "Kfim", "Kfny", "PTre", "PTim", "QTre", "QTimN",
           "accol", "wcre", "wcim", "Sre", "Sim", "Gre", "Gim", "PreS", "PimS",
           "abuf", "evs", "tscr", "tscr2", "aghold")



    def ffn_stage(tag):
        w1bf, Bw1 = sb("w1bf" + tag, (128, 8, FF), BF16, nslot=8)
        w2q, Bw2 = sb("w2q" + tag, (128, 32, D), FP8, nslot=4)
        hqT, Bhq = sb("hqT" + tag, (128, 32, 512), FP8, nslot=32)
        xfT, Bxf = sb("xfT" + tag, (128, 8, 512), BF16, nslot=4)
        b1c, Bb1c = sb("b1c" + tag, (128, 32), F32)
        b1row, Bb1row = sb("b1row" + tag, (32, 128), F32)
        b2q, Bb2q = sb("b2q" + tag, (1, D), F16)
        b2s, Bb2s = sb("b2s" + tag, (1, D), F32)
        onesh, Bonesh = sb("onesh" + tag, (1, 128), F16)
        evs16, Bevs16 = sb("evs16" + tag, (128, 2, D), F16, nslot=2)

        if tag == "c":
            w1r = dr["w1T"].rearrange("(c p) h -> p c h", p=128)
            for kdc in range(8):
                dma("g", w1bf[:, kdc, :], w1r[:, kdc, :], writes=[(Bw1, kdc)])
            for kdc in range(8):
                dma("y", w1sp[:, kdc, :], w1bf[:, kdc, :],
                    reads=[(Bw1, kdc)], writes=[Bw1sp])
        else:
            for kdc in range(8):
                dma("g", w1bf[:, kdc, :], w1sp[:, kdc, :],
                    reads=[Bw1sp], writes=[(Bw1, kdc)])
        dma("s", b1row[:], dr["b1"].rearrange("(hh p) -> hh p", p=128),
            writes=[Bb1row])
        p.emit("t", lambda e: e.matmul(
            pU[:, 0:32], b1row[:], c_id[0:32, 0:32],
            start=True, stop=True, is_transpose=True),
            reads=[Bb1row, Bcid], writes=[(BpU, 0)])
        p.emit("v", lambda e: e.tensor_copy(b1c[:], pU[:, 0:32]),
               reads=[(BpU, 0)], writes=[Bb1c])
        dma("s", b2s[:], dr["b2"][None, :], writes=[Bb2s])
        dma("g", onesh[:], dr["c_ones"], writes=[Bonesh])
        p.emit("s", lambda e: e.activation(out=b2q[:], in_=b2s[:],
                                           func=AF.Identity, bias=0.0, scale=1.0),
               reads=[Bb2s], writes=[Bb2q])
        if tag == "c":
            w2r = dr["w2T"].rearrange("(c p) d -> p c d", p=128)
            for jq in range(4):
                dma("g", w2q[:, 8 * jq:8 * (jq + 1), :],
                    w2r[:, 8 * jq:8 * (jq + 1), :], writes=[(Bw2, jq)])
            for jq in range(4):
                dma("y", w2sp[:, 8 * jq:8 * (jq + 1), :],
                    w2q[:, 8 * jq:8 * (jq + 1), :],
                    reads=[(Bw2, jq)], writes=[Bw2sp])
        else:
            for jq in range(4):
                dma("g", w2q[:, 8 * jq:8 * (jq + 1), :],
                    w2sp[:, 8 * jq:8 * (jq + 1), :],
                    reads=[Bw2sp], writes=[(Bw2, jq)])

        def c_prep(g4):
            tiles = list(range(4 * g4, 4 * g4 + 4))
            for li, i in enumerate(tiles):
                xt = stream[:, i, :]
                ln_stats_pair(i, 0, [(Bstream, i)], xt[:, 0:512], xt[:, 512:1024])
                slot = li % 2
                c = 8 * i
                p.emit("s", lambda e, c=c, xt=xt, slot=slot: e.activation(
                    out=xh[:, slot, :], in_=xt, func=AF.Identity,
                    bias=stat[:, c + 2:c + 3], scale=stat[:, c + 1:c + 2]),
                    reads=[(Bstream, i), (Bstat, i)], writes=[(Bxh, slot)])
                for kdc in range(8):
                    tp128(xh[:, slot, 128 * kdc:128 * (kdc + 1)], [(Bxh, slot)],
                          xfT[:, kdc, 128 * li:128 * (li + 1)], [(Bxf, li)], kdc)

        def c_ffn1(g4):
            for hh in range(32):
                po = slice(512 * (hh % 4), 512 * (hh % 4) + 512)
                for kdc in range(8):
                    p.emit("t", lambda e, hh=hh, kdc=kdc, po=po: e.matmul(
                        pX[:, po], w1bf[:, kdc, 128 * hh:128 * (hh + 1)],
                        xfT[:, kdc, :], start=(kdc == 0), stop=(kdc == 7)),
                        reads=[(Bw1, kdc)] + Bxf.all(),
                        writes=pb_(BpX, po) if kdc in (0, 7) else [])
                p.emit("s", lambda e, hh=hh, po=po: e.activation(
                    out=hqT[:, hh, :], in_=pX[:, po], func=AF.Relu,
                    bias=b1c[:, hh:hh + 1], scale=1.0),
                    reads=pb_(BpX, po) + [Bb1c], writes=[(Bhq, hh)])

        def c_ffn2(g4):
            tiles = list(range(4 * g4, 4 * g4 + 4))
            for li, i in enumerate(tiles):
                eslot = li % 2
                for nh2 in range(2):
                    sl = slice(512 * nh2, 512 * (nh2 + 1))
                    for hh in range(0, 32, 2):
                        p.emit("t", lambda e, hh=hh, li=li, sl=sl: e.matmul(
                            pY[:, sl],
                            hqT[:, hh:hh + 2, 128 * li:128 * (li + 1)],
                            w2q[:, hh:hh + 2, sl], start=(hh == 0), stop=False,
                            perf_mode=mybir.MatmulPerfMode.DoubleRow),
                            reads=[(Bhq, hh), (Bhq, hh + 1), (Bw2, hh // 8)],
                            writes=[(BpY, nh2)] if hh == 0 else [])
                    p.emit("t", lambda e, sl=sl, nh2=nh2: e.matmul(
                        pY[:, sl], onesh[:], b2q[:, sl], start=False, stop=True),
                        reads=[Bonesh, Bb2q], writes=[(BpY, nh2)])
                    p.emit("s", lambda e, sl=sl, eslot=eslot, nh2=nh2: e.activation(
                        out=evs16[:, eslot, sl], in_=pY[:, sl], func=AF.Identity,
                        bias=0.0, scale=1.0),
                        reads=[(BpY, nh2)], writes=[(Bevs16, eslot)])
                p.emit("g", lambda e, i=i, eslot=eslot: e.tensor_tensor(
                    out=stream[:, i, :], in0=stream[:, i, :],
                    in1=evs16[:, eslot, :], op=ALU.add),
                    reads=[(Bevs16, eslot), (Bstream, i)], writes=[(Bstream, i)])

        c_prep(0)
        c_ffn1(0)
        for g4 in range(1, 4):
            c_prep(g4)
            c_ffn2(g4 - 1)
            c_ffn1(g4)
        c_ffn2(3)

        sbfree("w1bf" + tag, "w2q" + tag, "hqT" + tag,
               "xfT" + tag, "b1c" + tag, "b1row" + tag, "b2q" + tag,
               "b2s" + tag, "onesh" + tag, "evs16" + tag)

    if n_stage >= 2:
        ffn_stage("c")

    if n_stage >= 3:
        QESC = float(DK) ** -0.5
        c_U2b, BcU2b = sb("c_U2b", (128, 128), F32R)
        c_mask2, Bcm2 = sb("c_mask2", (128, 128), F32)
        c_ind0, Bci0 = sb("c_ind0", (128, 1), F32R)
        c_ind1, Bci1 = sb("c_ind1", (128, 1), F32R)
        dma("g", c_U2b[:], dr["c_U2b"], writes=[BcU2b])
        dma("s", c_mask2[:], dr["c_mask2"], writes=[Bcm2])
        dma("g", c_ind0[:], dr["c_ind0"], writes=[Bci0])
        dma("g", c_ind1[:], dr["c_ind1"], writes=[Bci1])

        awc, Bawc = sb("awc", (128, 8), F32)
        awcrow, Bawcrow = sb("awcrow", (8, 128), F32)
        dma("s", awcrow[:], dr["attn_w"].rearrange("(c p) -> c p", p=128),
            writes=[Bawcrow])
        p.emit("t", lambda e: e.matmul(
            pU[:, 0:8], awcrow[:], c_id[0:8, 0:8],
            start=True, stop=True, is_transpose=True),
            reads=[Bawcrow, Bcid], writes=[(BpU, 0)])
        p.emit("v", lambda e: e.tensor_copy(awc[:], pU[:, 0:8]),
               reads=[(BpU, 0)], writes=[Bawc])

        BpUu = Buf("pUu", 2)
        kexpS, BkexpS = sb("kexpS", (128, NT, 512), BF16, nslot=NT)
        vS, BvS = sb("vS", (128, NT, D), BF16, nslot=NT)
        ebc, Bebc = sb("ebc", (128, NT, 8), F32, nslot=NT)
        Sgla, BSgla = sb("Sgla", (128, NH, DV), F32, nslot=NH)
        Sb16, BSb16 = sb("Sb16", (128, NH + 2, DV), BF16, nslot=NH + 2)
        rcrow, Brcrow = sb("rcrow", (1, NT, 128), BF16, nslot=NT)
        agh2, Bagh2 = sb("agh2", (128, D), F32)
        sigz, Bsigz = sb("sigz", (128, 512), F32, nslot=2)
        tscrD, BtscrD = sigz, Bsigz
        glcur, Bglcur = sb("glcur", (128, 512), F32R)
        expsc, Bexpsc = sb("expsc", (128, 512), F32)
        z1s, Bz1s = sb("z1s", (128, 16), BF16)
        z1T, Bz1T = sb("z1T", (16, 128), BF16)
        ebrow, Bebrow = sb("ebrow", (1, 512), F32)
        gcol, Bgcol = sb("gcol", (128, 2), F32)

        Wgk1b, BWgk1b = sb("Wgk1b", (128, 8, 16), BF16)
        Wgk2b, BWgk2b = sb("Wgk2b", (16, 512), BF16)
        bgk2r, Bbgk2r = sb("bgk2r", (1, 512), BF16)
        gnwc, Bgnwc = sb("gnwc", (128, 2), F32)
        gnwcrow, Bgnwcrow = sb("gnwcrow", (2, 128), F32)
        dma("s", gnwcrow[:], dr["gnw"].rearrange("(c p) -> c p", p=128),
            writes=[Bgnwcrow])
        p.emit("t", lambda e: e.matmul(
            pU[:, 0:2], gnwcrow[:], c_id[0:2, 0:2],
            start=True, stop=True, is_transpose=True),
            reads=[Bgnwcrow, Bcid], writes=[(BpU, 0)])
        p.emit("v", lambda e: e.tensor_copy(gnwc[:], pU[:, 0:2]),
               reads=[(BpU, 0)], writes=[Bgnwc])
        WqB, BWqB = sb("WqB", (128, 8, 512), BF16, nslot=8)
        WqBr = dr["Wq"].rearrange("(c p) o -> p c o", p=128)
        for kdc in range(8):
            dma("g", WqB[:, kdc, :], WqBr[:, kdc, :], writes=[(BWqB, kdc)])
        for kdc in range(8):
            ts("v", WqB[:, kdc, :], WqB[:, kdc, :], awc[:, kdc:kdc + 1], None,
               ALU.mult, ALU.bypass, [(BWqB, kdc), Bawc], [(BWqB, kdc)])

        WkB, BWkB = sb("WkB", (128, 8, 512), BF16, nslot=8)
        WvB, BWvB = sb("WvB", (128, 8, D), BF16, nslot=8)
        dma("g", Wgk1b[:], dr["Wgk1"].rearrange("(c p) o -> p c o", p=128),
            writes=[BWgk1b])
        dma("g", Wgk2b[:], dr["Wgk2"], writes=[BWgk2b])
        dma("g", bgk2r[:], dr["bgk2"][None, :], writes=[Bbgk2r])
        WkBr = dr["Wk"].rearrange("(c p) o -> p c o", p=128)
        for kdc in range(8):
            dma("g", WkB[:, kdc, :], WkBr[:, kdc, :], writes=[(BWkB, kdc)])
        WvBr = dr["Wv"].rearrange("(c p) o -> p c o", p=128)
        for kdc in range(8):
            dma("g", WvB[:, kdc, :], WvBr[:, kdc, :], writes=[(BWvB, kdc)])
        for kdc in range(8):
            ts("v", WkB[:, kdc, :], WkB[:, kdc, :], awc[:, kdc:kdc + 1], None,
               ALU.mult, ALU.bypass, [(BWkB, kdc), Bawc], [(BWkB, kdc)])
            ts("v", WvB[:, kdc, :], WvB[:, kdc, :], awc[:, kdc:kdc + 1], None,
               ALU.mult, ALU.bypass, [(BWvB, kdc), Bawc], [(BWvB, kdc)])
            ts("v", Wgk1b[:, kdc, :], Wgk1b[:, kdc, :], awc[:, kdc:kdc + 1], None,
               ALU.mult, ALU.bypass, [BWgk1b, Bawc], [BWgk1b])

        def d_stats(i):
            xt = stream[:, i, :]
            ln_stats_pair(i, 0, [(Bstream, i)], xt[:, 0:512], xt[:, 512:1024])
            c = 8 * i
            ts("v", stat[:, c + 5:c + 6], stat[:, c + 1:c + 2],
               stat[:, c + 1:c + 2], None, ALU.mult, ALU.bypass,
               [(Bstat, i)], [(Bstat, i)])
            ts("v", stat[:, c + 5:c + 6], stat[:, c + 5:c + 6], -1e-5, 1.0,
               ALU.mult, ALU.add, [(Bstat, i)], [(Bstat, i)])
            p.emit("s", lambda e, c=c: e.activation(
                out=stat[:, c:c + 1], in_=stat[:, c + 5:c + 6],
                func=AF.Ln, bias=RMS_EPS, scale=1.0),
                reads=[(Bstat, i), Bceps], writes=[(Bstat, i)])
            p.emit("s", lambda e, c=c: e.activation(
                out=stat[:, c + 3:c + 4], in_=stat[:, c:c + 1],
                func=AF.Exp, bias=0.0, scale=-0.5),
                reads=[(Bstat, i)], writes=[(Bstat, i)])
            p.emit("s", lambda e, c=c: e.activation(
                out=stat[:, c + 4:c + 5], in_=stat[:, c:c + 1],
                func=AF.Identity, bias=0.0, scale=-0.5),
                reads=[(Bstat, i)], writes=[(Bstat, i)])
            p.emit("s", lambda e, c=c: e.activation(
                out=stat[:, c + 5:c + 6], in_=stat[:, c:c + 1],
                func=AF.Exp, bias=0.0, scale=0.5),
                reads=[(Bstat, i)], writes=[(Bstat, i)])
            ts("v", stat[:, c:c + 1], stat[:, c + 3:c + 4], -1.0, None,
               ALU.mult, ALU.bypass, [(Bstat, i)], [(Bstat, i)])
            p.emit("t", lambda e, c=c: e.matmul(
                pU[0:1, 0:128], stat[:, c + 5:c + 6], c_id[:],
                start=True, stop=True, is_transpose=True),
                reads=[(Bstat, i), Bcid], writes=[(BpU, 0)])
            p.emit("v", lambda e, i=i: e.tensor_copy(
                rcrow[0:1, i, :], pU[0:1, 0:128]),
                reads=[(BpU, 0)], writes=[(Brcrow, i)])

        def d_h3T(i):
            norm_transpose(stream[:, i, :], i, 0, [(Bstream, i)], i % 2)

        def d_glog(i, want_expb):
            c = 8 * i
            tp = i % 2
            for kdc in range(8):
                p.emit("t", lambda e, kdc=kdc, tp=tp: e.matmul(
                    pX[:, 0:16], xhT[:, tp, kdc, :], Wgk1b[:, kdc, :],
                    start=(kdc == 0), stop=(kdc == 7)),
                    reads=[(BxhT, tp), BWgk1b],
                    writes=[(BpX, 0)] if kdc in (0, 7) else [])
            p.emit("s", lambda e: e.activation(
                out=z1s[:], in_=pX[:, 0:16], func=AF.Identity, bias=0.0, scale=1.0),
                reads=[(BpX, 0)], writes=[Bz1s])
            p.emit("t", lambda e: e.matmul(
                pTb[0:16, 0:128], z1s[:], c_idb[:],
                start=True, stop=True, is_transpose=True),
                reads=[Bz1s, Bcidb], writes=[(BpU, 0)])
            p.emit("v", lambda e: e.tensor_copy(z1T[:], pTb[0:16, 0:128]),
                   reads=[(BpU, 0)], writes=[Bz1T])
            p.emit("t", lambda e: e.matmul(
                pX[:, 0:512], z1T[:], Wgk2b[:], start=True, stop=False),
                reads=[Bz1T, BWgk2b], writes=[(BpX, 0)])
            p.emit("t", lambda e, i=i: e.matmul(
                pX[:, 0:512], rcrow[0:1, i, :], bgk2r[:], start=False, stop=True),
                reads=[(Brcrow, i), Bbgk2r], writes=[(BpX, 0)])
            p.emit("s", lambda e, c=c: e.activation(
                out=sigz[:], in_=pX[:, 0:512], func=AF.Exp,
                bias=0.0, scale=stat[:, c:c + 1]),
                reads=[(BpX, 0), (Bstat, i)], writes=[(Bsigz, 0), (Bsigz, 1)])
            p.emit("s", lambda e: e.activation(
                out=glcur[:], in_=sigz[:], func=AF.Ln, bias=1.0, scale=1.0),
                reads=[(Bsigz, 0), (Bsigz, 1)], writes=[Bglcur])
            p.emit("t", lambda e: e.matmul(
                pX[:, 512:1024], c_U2b[:], glcur[:], start=True, stop=True),
                reads=[BcU2b, Bglcur], writes=[(BpX, 1)])
            sgn = 1.0 if want_expb else -1.0
            p.emit("s", lambda e, c=c, sgn=sgn: e.activation(
                out=expsc[:], in_=pX[:, 512:1024], func=AF.Exp,
                bias=stat[:, c + 4:c + 5], scale=sgn),
                reads=[(BpX, 1), (Bstat, i)], writes=[Bexpsc])

        def d_eblast(i):
            for ci, (cind, bci) in enumerate(((c_ind0, Bci0), (c_ind1, Bci1))):
                p.emit("t", lambda e, cind=cind, ci=ci, bci=bci: e.matmul(
                    pX[0:1, 512 * ci:512 * (ci + 1)], cind[:], glcur[:],
                    start=True, stop=True),
                    reads=[bci, Bglcur], writes=[(BpX, ci)])
            for ci in range(2):
                p.emit("s", lambda e, ci=ci: e.activation(
                    out=ebrow[:], in_=pX[0:1, 512 * ci:512 * (ci + 1)],
                    func=AF.Exp, bias=0.0, scale=1.0),
                    reads=[(BpX, ci)], writes=[Bebrow])
                for j2 in range(4):
                    p.emit("t", lambda e, j2=j2: e.matmul(
                        pU[:, 0:1], ebrow[0:1, 128 * j2:128 * (j2 + 1)],
                        c_id[0:1, 0:1], start=True, stop=True, is_transpose=True),
                        reads=[Bebrow, Bcid], writes=[(BpU, 0)])
                    p.emit("v", lambda e, j2=j2, ci=ci, i=i: e.tensor_copy(
                        ebc[:, i, 4 * ci + j2:4 * ci + j2 + 1], pU[:, 0:1]),
                        reads=[(BpU, 0)], writes=[(Bebc, i)])

        def p1_chain(i):
            for ch2 in range(2):
                rs = slice(64 * ch2, 64 * (ch2 + 1))
                for h in range(NH):
                    hp = h % 2
                    usl = slice(256 + 512 * hp, 512 + 512 * hp)
                    tsl = slice(256 * hp, 256 * (hp + 1))
                    p.emit("t", lambda e, rs=rs, h=h, i=i, usl=usl: e.matmul(
                        pU[:, usl], kexpS[rs, i, 128 * h:128 * (h + 1)],
                        vS[rs, i, 256 * h:256 * (h + 1)], start=True, stop=True),
                        reads=[(BkexpS, i), (BvS, i)], writes=[(BpUu, hp)])
                    tt("v", tscrD[:, tsl], Sgla[:, h, :], pU[:, usl],
                       ALU.add, [(BSgla, h), (BpUu, hp)], [(BtscrD, hp)])
                    p.emit("s", lambda e, h=h, i=i, ch2=ch2, tsl=tsl:
                           e.activation(
                               out=Sgla[:, h, :], in_=tscrD[:, tsl],
                               func=AF.Identity, bias=0.0,
                               scale=ebc[:, i, 4 * ch2 + h:4 * ch2 + h + 1]),
                           reads=[(BtscrD, hp), (Bebc, i)], writes=[(BSgla, h)])

        for h in range(NH):
            p.emit("g", lambda e, h=h: e.memset(Sgla[:, h, :], 0.0),
                   writes=[(BSgla, h)])
        d_stats(0)
        d_h3T(0)
        for i in range(NT):
            d_glog(i, want_expb=False)
            d_eblast(i)
            if i + 1 < NT:
                d_stats(i + 1)
            for kdc in range(8):
                p.emit("t", lambda e, kdc=kdc, i=i: e.matmul(
                    pX[:, 1024:1536], xhT[:, i % 2, kdc, :], WkB[:, kdc, :],
                    start=(kdc == 0), stop=(kdc == 7)),
                    reads=[(BxhT, i % 2), (BWkB, kdc)],
                    writes=[(BpX, 2)] if kdc in (0, 7) else [])
            p.emit("v", lambda e, i=i: e.tensor_tensor(
                out=kexpS[:, i, :], in0=pX[:, 1024:1536], in1=expsc[:],
                op=ALU.mult),
                reads=[(BpX, 2), Bexpsc], writes=[(BkexpS, i)])
            if i + 1 < NT:
                d_h3T(i + 1)
            c = 8 * i
            for nh2 in range(2):
                sl = slice(512 * nh2, 512 * (nh2 + 1))
                for kdc in range(8):
                    p.emit("t", lambda e, kdc=kdc, sl=sl, nh2=nh2, i=i: e.matmul(
                        pY[:, sl], xhT[:, i % 2, kdc, :], WvB[:, kdc, sl],
                        start=(kdc == 0), stop=(kdc == 7)),
                        reads=[(BxhT, i % 2), (BWvB, kdc)],
                        writes=[(BpY, nh2)] if kdc in (0, 7) else [])
                p.emit("s", lambda e, c=c, sl=sl, i=i, nh2=nh2: e.activation(
                    out=vS[:, i, sl], in_=pY[:, sl], func=AF.Identity,
                    bias=0.0, scale=stat[:, c + 3:c + 4]),
                    reads=[(BpY, nh2), (Bstat, i)], writes=[(BvS, i)])
            if i > 0:
                p1_chain(i - 1)
        p1_chain(NT - 1)

        for h in range(NH):
            dma("g", ag2_in[:, 256 * h:256 * (h + 1)], Sgla[:, h, :],
                reads=[(BSgla, h)])
        cc2 = p.emit("g", lambda e: e.collective_compute(
            "AllGather", ALU.bypass, ins=[ag2_in.opt()], outs=[ag2_out.opt()],
            replica_groups=[list(range(NCORE))]),
            extra_waits=[("dg" + str(i), p.counts["dg" + str(i)])
                         for i in range(4)],
            is_collective=True)
        for h in range(NH):
            p.emit("g", lambda e, h=h: e.memset(Sgla[:, h, :], 0.0),
                   writes=[(BSgla, h)])
        for r in range(NCORE):
            p.emit("g", lambda e, r=r: e.dma_start(
                out=agh2[:], in_=ag2_out[128 * r:128 * (r + 1), :]),
                writes=[Bagh2], extra_waits=[cc2], dma=True)
            for h in range(NH):
                stt("v", Sgla[:, h, :], agh2[:, 256 * h:256 * (h + 1)],
                    selcs[:, r:r + 1], Sgla[:, h, :], ALU.mult, ALU.add,
                    [Bagh2, Bsel, (BSgla, h)], [(BSgla, h)])
        for h in range(NH):
            p.emit("v", lambda e, h=h: e.tensor_copy(
                Sb16[:, h, :], Sgla[:, h, :]),
                reads=[(BSgla, h)], writes=[(BSb16, h)])

        spillT = spill.rearrange("(tt p) d -> p tt d", p=128)

        sbfree("WkB", "WvB")
        WgB, BWgB = sb("WgB", (128, 8, D), BF16, nslot=8)
        WoB, BWoB = sb("WoB", (128, 8, D), BF16, nslot=8)
        qf, Bqf = sb("qf", (128, NH, 128), BF16)
        qeT0, BqeT0 = sb("qeT0", (128, 2, 128), BF16, nslot=2)
        qeT1, BqeT1 = sb("qeT1", (128, 2, 128), BF16, nslot=2)
        kexpT, BkexpT = sb("kexpT", (128, 2, 128), BF16, nslot=2)
        Atm, BAtm = sb("Atm", (128, 2, 128), BF16, nslot=2)
        silug, Bsilug = sb("silug", (128, D), F16)
        obuf, Bobuf = sb("obuf", (128, NH, 256), BF16, nslot=NH)
        oT, BoT = sb("oT", (128, 8, 128), BF16)
        gns, Bgns = sb("gns", (128, 2, 8), F32, nslot=2)
        o2s, Bo2s = sb("o2s", (128, 2, 256), BF16, nslot=2)
        ob32, Bob32 = sb("ob32", (128, 2, 256), F32, nslot=2)
        x2bt, Bx2bt = agh2, Bagh2
        WgBr = dr["Wg"].rearrange("(c p) o -> p c o", p=128)
        for kdc in range(8):
            dma("g", WgB[:, kdc, :], WgBr[:, kdc, :], writes=[(BWgB, kdc)])
        WoBr = dr["Wo"].rearrange("(c p) o -> p c o", p=128)
        for kdc in range(8):
            dma("g", WoB[:, kdc, :], WoBr[:, kdc, :], writes=[(BWoB, kdc)])
        for kdc in range(8):
            ts("v", WgB[:, kdc, :], WgB[:, kdc, :], awc[:, kdc:kdc + 1], None,
               ALU.mult, ALU.bypass, [(BWgB, kdc), Bawc], [(BWgB, kdc)])
            ts("v", WoB[:, kdc, :], WoB[:, kdc, :], gnwc[:, kdc % 2:kdc % 2 + 1],
               None, ALU.mult, ALU.bypass, [(BWoB, kdc), Bgnwc], [(BWoB, kdc)])

        for s2 in range(2):
            p.emit("g", lambda e, s2=s2: e.memset(qeT0[:, s2, 64:128], 0.0),
                   writes=[(BqeT0, s2)])
            p.emit("g", lambda e, s2=s2: e.memset(qeT1[:, s2, 0:64], 0.0),
                   writes=[(BqeT1, s2)])

        def d_front(i):
            c = 8 * i
            d_h3T(i)
            d_glog(i, want_expb=True)
            tp = i % 2
            for h in range(NH):
                for kdc in range(8):
                    p.emit("t", lambda e, kdc=kdc, h=h, tp=tp: e.matmul(
                        pX[:, 1024 + 128 * h:1152 + 128 * h],
                        WqB[:, kdc, 128 * h:128 * (h + 1)], xhT[:, tp, kdc, :],
                        start=(kdc == 0), stop=(kdc == 7)),
                        reads=[(BWqB, kdc), (BxhT, tp)],
                        writes=[(BpX, 2)] if kdc in (0, 7) else [])
                p.emit("s", lambda e, h=h: e.activation(
                    out=qf[:, h, :], in_=pX[:, 1024 + 128 * h:1152 + 128 * h],
                    func=AF.Identity, bias=0.0, scale=1.0),
                    reads=[(BpX, 2)], writes=[Bqf])
            for nh2 in range(2):
                sl = slice(512 * nh2, 512 * (nh2 + 1))
                for kdc in range(8):
                    p.emit("t", lambda e, kdc=kdc, sl=sl, tp=tp: e.matmul(
                        pX[:, sl], xhT[:, tp, kdc, :], WgB[:, kdc, sl],
                        start=(kdc == 0), stop=(kdc == 7)),
                        reads=[(BxhT, tp), (BWgB, kdc)],
                        writes=[(BpX, nh2)] if kdc in (0, 7) else [])
                p.emit("s", lambda e, c=c, sl=sl, nh2=nh2: e.activation(
                    out=silug[:, sl], in_=pX[:, sl], func=AF.Silu,
                    bias=0.0, scale=stat[:, c + 3:c + 4]),
                    reads=[(BpX, nh2), (Bstat, i)], writes=[Bsilug])

        def d_head_front(i, h):
            hp = h % 2
            osl = slice(512 * hp, 512 * hp + 256)
            p.emit("t", lambda e, h=h: e.matmul(
                pU[:, 0:128], expsc[:, 128 * h:128 * (h + 1)], c_id[:],
                start=True, stop=True, is_transpose=True),
                reads=[Bexpsc, Bcid], writes=[(BpU, 0)])
            p.emit("v", lambda e, h=h, hp=hp: e.scalar_tensor_tensor(
                out=qeT0[:, hp, 0:64], in0=pU[:, 0:64], scalar=QESC,
                in1=qf[:, h, 0:64], op0=ALU.mult, op1=ALU.mult),
                reads=[(BpU, 0), Bqf], writes=[(BqeT0, hp)])
            p.emit("v", lambda e, h=h, hp=hp: e.scalar_tensor_tensor(
                out=qeT1[:, hp, 64:128], in0=pU[:, 64:128], scalar=QESC,
                in1=qf[:, h, 64:128], op0=ALU.mult, op1=ALU.mult),
                reads=[(BpU, 0), Bqf], writes=[(BqeT1, hp)])
            p.emit("t", lambda e, h=h, i=i: e.matmul(
                pTb[:, 1024:1152], kexpS[:, i, 128 * h:128 * (h + 1)],
                c_idb[:], start=True, stop=True, is_transpose=True),
                reads=[(BkexpS, i), Bcidb], writes=[(BpU, 1)])
            p.emit("v", lambda e, hp=hp: e.tensor_copy(
                kexpT[:, hp, :], pTb[:, 1024:1152]),
                reads=[(BpU, 1)], writes=[(BkexpT, hp)])
            asl = slice(1536 + 128 * hp, 1664 + 128 * hp)
            p.emit("t", lambda e, hp=hp, asl=asl: e.matmul(
                pX[:, asl], kexpT[:, hp, :], qeT0[:, hp, :],
                start=True, stop=False),
                reads=[(BqeT0, hp), (BkexpT, hp)], writes=[(BpX, 3)])
            p.emit("t", lambda e, hp=hp, asl=asl: e.matmul(
                pX[:, asl], kexpT[:, hp, :], qeT1[:, hp, :],
                start=False, stop=True),
                reads=[(BkexpT, hp), (BqeT1, hp)], writes=[(BpX, 3)])
            p.emit("v", lambda e, hp=hp, asl=asl: e.tensor_tensor(
                out=Atm[:, hp, :], in0=pX[:, asl], in1=c_mask2[:], op=ALU.mult),
                reads=[(BpX, 3), Bcm2], writes=[(BAtm, hp)])
            p.emit("t", lambda e, h=h, i=i: e.matmul(
                pU[:, 256:512], kexpS[0:64, i, 128 * h:128 * (h + 1)],
                vS[0:64, i, 256 * h:256 * (h + 1)], start=True, stop=True),
                reads=[(BkexpS, i), (BvS, i)], writes=[(BpUu, 0)])
            p.emit("t", lambda e, h=h, i=i: e.matmul(
                pU[:, 768:1024], kexpS[64:128, i, 128 * h:128 * (h + 1)],
                vS[64:128, i, 256 * h:256 * (h + 1)], start=True, stop=True),
                reads=[(BkexpS, i), (BvS, i)], writes=[(BpUu, 1)])
            p.emit("t", lambda e, h=h, i=i, hp=hp, osl=osl: e.matmul(
                pY[:, osl], Atm[:, hp, :], vS[:, i, 256 * h:256 * (h + 1)],
                start=True, stop=False),
                reads=[(BAtm, hp), (BvS, i)], writes=[(BpY, hp)])
            p.emit("t", lambda e, h=h, hp=hp, osl=osl: e.matmul(
                pY[:, osl], qeT0[:, hp, :], Sb16[:, h, :],
                start=False, stop=True),
                reads=[(BqeT0, hp), (BSb16, h)], writes=[(BpY, hp)])
            tsl = slice(256 * hp, 256 * (hp + 1))
            tt("v", tscrD[:, tsl], Sgla[:, h, :], pU[:, 256:512],
               ALU.add, [(BSgla, h), (BpUu, 0)], [(BtscrD, hp)])
            p.emit("s", lambda e, h=h, i=i, tsl=tsl: e.activation(
                out=Sgla[:, h, :], in_=tscrD[:, tsl], func=AF.Identity,
                bias=0.0, scale=ebc[:, i, h:h + 1]),
                reads=[(BtscrD, hp), (Bebc, i)], writes=[(BSgla, h)])
            p.emit("s", lambda e, h=h, hp=hp: e.activation(
                out=Sb16[:, NH + hp, :], in_=Sgla[:, h, :], func=AF.Identity,
                bias=0.0, scale=1.0),
                reads=[(BSgla, h)], writes=[(BSb16, NH + hp)])
            tt("v", tscrD[:, tsl], Sgla[:, h, :], pU[:, 768:1024],
               ALU.add, [(BSgla, h), (BpUu, 1)], [(BtscrD, hp)])
            p.emit("s", lambda e, h=h, i=i, tsl=tsl: e.activation(
                out=Sgla[:, h, :], in_=tscrD[:, tsl], func=AF.Identity,
                bias=0.0, scale=ebc[:, i, 4 + h:4 + h + 1]),
                reads=[(BtscrD, hp), (Bebc, i)], writes=[(BSgla, h)])
            p.emit("s", lambda e, h=h: e.activation(
                out=Sb16[:, h, :], in_=Sgla[:, h, :], func=AF.Identity,
                bias=0.0, scale=1.0),
                reads=[(BSgla, h)], writes=[(BSb16, h)])

        def d_head_close(i, h):
            hp = h % 2
            osl = slice(512 * hp, 512 * hp + 256)
            o2l = slice(512 * hp + 256, 512 * hp + 512)
            p.emit("t", lambda e, h=h, hp=hp, o2l=o2l: e.matmul(
                pY[:, o2l], qeT1[:, hp, :], Sb16[:, NH + hp, :],
                start=True, stop=True),
                reads=[(BqeT1, hp), (BSb16, NH + hp)], writes=[(BpY, hp)])
            p.emit("s", lambda e, hp=hp, o2l=o2l: e.activation(
                out=o2s[:, hp, :], in_=pY[:, o2l], func=AF.Identity,
                bias=0.0, scale=1.0),
                reads=[(BpY, hp)], writes=[(Bo2s, hp)])
            tt("v", ob32[:, hp, :], o2s[:, hp, :], pY[:, osl], ALU.add,
               [(Bo2s, hp), (BpY, hp)], [(Bob32, hp)])
            p.emit("v", lambda e, hp=hp: e.bn_stats(
                gns[:, hp, 0:6], ob32[:, hp, :]),
                reads=[(Bob32, hp)], writes=[(Bgns, hp)])
            p.emit("v", lambda e, hp=hp: e.bn_aggr(
                gns[:, hp, 6:8], gns[:, hp, 0:6]),
                reads=[(Bgns, hp)], writes=[(Bgns, hp)])
            p.emit("v", lambda e, hp=hp: e.tensor_scalar(
                out=gns[:, hp, 0:1], in0=gns[:, hp, 6:7],
                scalar1=gns[:, hp, 6:7], scalar2=gns[:, hp, 7:8],
                op0=ALU.mult, op1=ALU.add),
                reads=[(Bgns, hp)], writes=[(Bgns, hp)])
            p.emit("s", lambda e, hp=hp: e.activation(
                out=gns[:, hp, 1:2], in_=gns[:, hp, 0:1], func=AF.Ln,
                bias=RMS_EPS, scale=1.0),
                reads=[(Bgns, hp), Bceps], writes=[(Bgns, hp)])
            p.emit("s", lambda e, hp=hp: e.activation(
                out=gns[:, hp, 2:3], in_=gns[:, hp, 1:2], func=AF.Exp,
                bias=0.0, scale=-0.5),
                reads=[(Bgns, hp)], writes=[(Bgns, hp)])
            p.emit("v", lambda e, h=h, hp=hp:
                   e.scalar_tensor_tensor(
                       out=obuf[:, h, :], in0=ob32[:, hp, :],
                       scalar=gns[:, hp, 2:3],
                       in1=silug[:, 256 * h:256 * (h + 1)],
                       op0=ALU.mult, op1=ALU.mult),
                   reads=[(Bob32, hp), (Bgns, hp), Bsilug],
                   writes=[(Bobuf, h)])

        d_front(0)
        _DEFER = False
        for i in range(NT):
            c = 8 * i
            for h in range(NH):
                d_head_front(i, h)
                if not _DEFER:
                    d_head_close(i, h)
                elif h > 0:
                    d_head_close(i, h - 1)
            if _DEFER:
                d_head_close(i, NH - 1)
            if i + 1 < NT:
                d_front(i + 1)
            oflat = obuf[:].rearrange("p h e -> p (h e)")
            for kdc in range(8):
                tp128(oflat[:, 128 * kdc:128 * (kdc + 1)], [(Bobuf, kdc // 2)],
                      oT[:, kdc, :], [BoT], kdc)
            for nh2 in range(2):
                sl = slice(512 * nh2, 512 * (nh2 + 1))
                for kdc in range(8):
                    p.emit("t", lambda e, kdc=kdc, sl=sl, nh2=nh2: e.matmul(
                        pY[:, sl], oT[:, kdc, :], WoB[:, kdc, sl],
                        start=(kdc == 0), stop=(kdc == 7)),
                        reads=[BoT, (BWoB, kdc)],
                        writes=[(BpY, nh2)] if kdc in (0, 7) else [])
            stt("v", x2bt[:], stream[:, i, :], stat[:, c + 1:c + 2],
                pY[:, 0:1024], ALU.mult, ALU.add,
                [(Bstream, i), (Bstat, i), (BpY, 0), (BpY, 1)], [Bx2bt])
            ts("v", x2bt[:], x2bt[:], stat[:, c + 2:c + 3], None,
               ALU.add, ALU.bypass, [Bx2bt, (Bstat, i)], [Bx2bt])
            sl2 = i % 2
            p.emit("v", lambda e, sl2=sl2: e.bn_stats(
                bnscr[:, sl2, 0:6], x2bt[:, 0:512]),
                reads=[Bx2bt], writes=[(Bbn, sl2)])
            p.emit("v", lambda e, sl2=sl2: e.bn_stats(
                bnscr[:, sl2, 6:12], x2bt[:, 512:1024]),
                reads=[Bx2bt], writes=[(Bbn, sl2)])
            p.emit("v", lambda e, sl2=sl2: e.bn_aggr(
                mvt[:, sl2, :], bnscr[:, sl2, :]),
                reads=[(Bbn, sl2)], writes=[(Bmvt, sl2)])
            p.emit("v", lambda e, c=c, sl2=sl2: e.tensor_scalar(
                out=stat[:, c + 6:c + 7], in0=mvt[:, sl2, 0:1],
                scalar1=mvt[:, sl2, 0:1], scalar2=mvt[:, sl2, 1:2],
                op0=ALU.mult, op1=ALU.add),
                reads=[(Bmvt, sl2)], writes=[(Bstat, i)])
            p.emit("s", lambda e, c=c: e.activation(
                out=stat[:, c + 6:c + 7], in_=stat[:, c + 6:c + 7],
                func=AF.Ln, bias=RMS_EPS, scale=1.0),
                reads=[(Bstat, i), Bceps], writes=[(Bstat, i)])
            p.emit("s", lambda e, c=c: e.activation(
                out=stat[:, c + 7:c + 8], in_=stat[:, c + 6:c + 7],
                func=AF.Exp, bias=0.0, scale=-0.5),
                reads=[(Bstat, i)], writes=[(Bstat, i)])
            slot = i % 2
            p.emit("s", lambda e, c=c, slot=slot: e.activation(
                out=xh[:, slot, :], in_=x2bt[:], func=AF.Identity,
                bias=0.0, scale=stat[:, c + 7:c + 8]),
                reads=[Bx2bt, (Bstat, i)], writes=[(Bxh, slot)])
            dma("y", spillT[:, i, :], xh[:, slot, :], reads=[(Bxh, slot)])
            p.emit("g", lambda e, i=i: e.tensor_tensor(
                out=stream[:, i, :], in0=stream[:, i, :], in1=x2bt[:],
                op=ALU.add),
                reads=[Bx2bt, (Bstream, i)], writes=[(Bstream, i)])

        sbfree("gnwc", "gnwcrow", "WqB", "WgB", "WoB", "qf", "qeT0", "qeT1", "kexpT", "Atm",
               "silug", "obuf", "oT", "gns", "o2s", "ob32",
               "kexpS", "vS", "ebc", "Sgla", "Sb16", "rcrow", "agh2",
               "sigz", "glcur", "expsc", "z1s", "z1T",
               "ebrow", "gcol", "Wgk1b", "Wgk2b", "bgk2r", "awc", "awcrow",
               "c_U2b", "c_mask2", "c_ind0", "c_ind1")

        mwc, Bmwc = sb("mwc", (128, 8), F32)
        mwcrow, Bmwcrow = sb("mwcrow", (8, 128), F32)
        dma("s", mwcrow[:], dr["mlp_w"].rearrange("(c p) -> c p", p=128),
            writes=[Bmwcrow])
        p.emit("t", lambda e: e.matmul(
            pU[:, 0:8], mwcrow[:], c_id[0:8, 0:8],
            start=True, stop=True, is_transpose=True),
            reads=[Bmwcrow, Bcid], writes=[(BpU, 0)])
        p.emit("v", lambda e: e.tensor_copy(mwc[:], pU[:, 0:8]),
               reads=[(BpU, 0)], writes=[Bmwc])
        MIH = MI // 2
        WgaB, BWgaB = sb("WgaB", (128, 8, MIH), BF16, nslot=8)
        WupB, BWupB = sb("WupB", (128, 8, MIH), BF16, nslot=8)
        WdoB, BWdoB = sb("WdoB", (128, 11, D), BF16, nslot=11)
        sgb2, Bsgb2 = sb("sgb2", (128, 2, 512), F16, nslot=2)
        mtl2, Bmtl2 = sb("mtl2", (128, 2, 512), BF16, nslot=2)
        mtT2, BmtT2 = sb("mtT2", (128, 2, 4, 128), BF16, nslot=2)

        def mlp_reload(i):
            sl2 = i % 2
            dma("y", xh[:, sl2, :], spillT[:, i, :], writes=[(Bxh, sl2)])

        def mlp_tp(i):
            sl2 = i % 2
            for kdc in range(8):
                tp128(xh[:, sl2, 128 * kdc:128 * (kdc + 1)], [(Bxh, sl2)],
                      xhT[:, sl2, kdc, :], [(BxhT, sl2)], kdc)

        for fh in range(2):
            fsl = slice(MIH * fh, MIH * (fh + 1))
            Wgar = dr["Wgate"][:, fsl].rearrange("(c p) o -> p c o", p=128)
            Wupr = dr["Wup"][:, fsl].rearrange("(c p) o -> p c o", p=128)
            Wdor = dr["Wdown"][fsl, :].rearrange("(c p) o -> p c o", p=128)
            for kdc in range(8):
                dma("g", WgaB[:, kdc, :], Wgar[:, kdc, :], writes=[(BWgaB, kdc)])
                dma("g", WupB[:, kdc, :], Wupr[:, kdc, :], writes=[(BWupB, kdc)])
            for fc in range(11):
                dma("g", WdoB[:, fc, :], Wdor[:, fc, :], writes=[(BWdoB, fc)])
            for kdc in range(8):
                ts("v", WgaB[:, kdc, :], WgaB[:, kdc, :], mwc[:, kdc:kdc + 1],
                   None, ALU.mult, ALU.bypass, [(BWgaB, kdc), Bmwc],
                   [(BWgaB, kdc)])
                ts("v", WupB[:, kdc, :], WupB[:, kdc, :], mwc[:, kdc:kdc + 1],
                   None, ALU.mult, ALU.bypass, [(BWupB, kdc), Bmwc],
                   [(BWupB, kdc)])
            NBK = [512, 512, 384]
            mlp_reload(0)
            mlp_tp(0)
            for i in range(NT):
                sl2 = i % 2
                off = 0
                for nb, w in enumerate(NBK):
                    np_ = nb % 2
                    slq = slice(off, off + w)
                    gsl = slice(512 * np_, 512 * np_ + w)
                    usl_ = slice(1024 + 512 * np_, 1024 + 512 * np_ + w)
                    for kdc in range(8):
                        p.emit("t", lambda e, kdc=kdc, slq=slq, gsl=gsl,
                               sl2=sl2: e.matmul(
                                   pX[:, gsl], xhT[:, sl2, kdc, :],
                                   WgaB[:, kdc, slq],
                                   start=(kdc == 0), stop=(kdc == 7)),
                               reads=[(BxhT, sl2), (BWgaB, kdc)],
                               writes=pb_(BpX, gsl) if kdc in (0, 7) else [])
                    if nb == 0 and i + 1 < NT:
                        mlp_reload(i + 1)
                    for kdc in range(8):
                        p.emit("t", lambda e, kdc=kdc, slq=slq, usl_=usl_,
                               sl2=sl2: e.matmul(
                                   pX[:, usl_], xhT[:, sl2, kdc, :],
                                   WupB[:, kdc, slq],
                                   start=(kdc == 0), stop=(kdc == 7)),
                               reads=[(BxhT, sl2), (BWupB, kdc)],
                               writes=pb_(BpX, usl_) if kdc in (0, 7) else [])
                    p.emit("s", lambda e, w=w, np_=np_, gsl=gsl: e.activation(
                        out=sgb2[:, np_, 0:w], in_=pX[:, gsl], func=AF.Silu,
                        bias=0.0, scale=1.0),
                        reads=pb_(BpX, gsl), writes=[(Bsgb2, np_)])
                    p.emit("v", lambda e, w=w, np_=np_, usl_=usl_:
                           e.tensor_tensor(
                               out=mtl2[:, np_, 0:w], in0=pX[:, usl_],
                               in1=sgb2[:, np_, 0:w], op=ALU.mult),
                           reads=pb_(BpX, usl_) + [(Bsgb2, np_)],
                           writes=[(Bmtl2, np_)])
                    if nb == 1 and i + 1 < NT:
                        mlp_tp(i + 1)
                    nch = w // 128
                    for fc in range(nch):
                        tp128(mtl2[:, np_, 128 * fc:128 * (fc + 1)],
                              [(Bmtl2, np_)],
                              mtT2[:, np_, fc, :], [(BmtT2, np_)], fc)
                    for nh2 in range(2):
                        sl = slice(512 * nh2, 512 * (nh2 + 1))
                        for fc in range(nch):
                            gfc = off // 128 + fc
                            p.emit("t", lambda e, fc=fc, gfc=gfc, sl=sl, nb=nb,
                                   np_=np_:
                                   e.matmul(pY[:, sl], mtT2[:, np_, fc, :],
                                            WdoB[:, gfc, sl],
                                            start=(nb == 0 and fc == 0),
                                            stop=(nb == len(NBK) - 1 and
                                                  fc == nch - 1)),
                                   reads=[(BmtT2, np_), (BWdoB, gfc)],
                                   writes=[(BpY, nh2)]
                                   if (nb == 0 and fc == 0) or
                                      (nb == len(NBK) - 1 and fc == nch - 1)
                                   else [])
                    off += w
                p.emit("v", lambda e, i=i: e.tensor_tensor(
                    out=stream[:, i, :], in0=stream[:, i, :], in1=pY[:, 0:1024],
                    op=ALU.add),
                    reads=[(BpY, 0), (BpY, 1), (Bstream, i)],
                    writes=[(Bstream, i)])
        sbfree("WgaB", "WupB", "WdoB", "sgb2", "mtl2", "mtT2", "mwc", "mwcrow")

    if n_stage >= 4:
        ffn_stage("e")

    for i in range(NT):
        dma("y", outT[:, i, :], stream[:, i, :], reads=[(Bstream, i)])
    p.wait_only("y", extra_waits=[("dy" + str(i), p.counts["dy" + str(i)])
                                  for i in range(4)])
    p.replay()
    return nc


def kernel(**inputs):
    n_stage = int(inputs.pop("_n_stage", N_STAGE))
    key = ("nc", n_stage)
    if key not in _CACHE:
        _CACHE[key] = build_nc(n_stage)
    nc = _CACHE[key]
    consts = _host_consts()

    x = np.ascontiguousarray(np.asarray(inputs["L0_attn_a_x"], dtype=np.float32))
    base = {
        "W_inT": inputs["L0_attn_a_W_in"].T,
        "b_in": inputs["L0_attn_a_b_in"],
        "gamma": inputs["L0_attn_a_gamma"], "beta": inputs["L0_attn_a_beta"],
        "A_re": np.concatenate([inputs["L0_attn_a_A_re"], [1.0, 0.0]]),
        "A_im": np.concatenate([inputs["L0_attn_a_A_im"], [0.0, 0.0]]),
        "B_re": np.concatenate([inputs["L0_attn_a_B_re"], [1.0, 0.0]]),
        "B_im": np.concatenate([inputs["L0_attn_a_B_im"], [0.0, 0.0]]),
        "CreT": np.concatenate([inputs["L0_attn_a_C_re"].T,
                                np.asarray(inputs["L0_attn_a_D_re"],
                                           dtype=np.float32)[None, :],
                                np.zeros((1, D), dtype=np.float32)], axis=0),
        "CimT": np.concatenate([inputs["L0_attn_a_C_im"].T,
                                np.asarray(inputs["L0_attn_a_D_im"],
                                           dtype=np.float32)[None, :],
                                np.zeros((1, D), dtype=np.float32)], axis=0),
        "w1T": inputs["L0_ffn_w1"].T, "b1": inputs["L0_ffn_b1"],
        "w2T": inputs["L0_ffn_w2"].T, "b2": inputs["L0_ffn_b2"],
        "attn_w": inputs["L0_attn_b_attn_norm_w"],
        "Wq": inputs["L0_attn_b_Wq"], "Wk": inputs["L0_attn_b_Wk"],
        "Wv": inputs["L0_attn_b_Wv"], "Wg": inputs["L0_attn_b_Wg"],
        "Wgk1": inputs["L0_attn_b_Wgk1"], "Wgk2": inputs["L0_attn_b_Wgk2"],
        "bgk2": inputs["L0_attn_b_bgk2"], "gnw": inputs["L0_attn_b_gnorm_w"],
        "Wo": inputs["L0_attn_b_Wo"], "mlp_w": inputs["L0_attn_b_mlp_norm_w"],
        "Wgate": inputs["L0_attn_b_W_gate"], "Wup": inputs["L0_attn_b_W_up"],
        "Wdown": inputs["L0_attn_b_W_down"],
    }
    base = {k: np.ascontiguousarray(np.asarray(v, dtype=np.float32))
            for k, v in base.items()}
    base.update(consts)

    in_maps = []
    for c in range(NCORE):
        b, h = c // 2, c % 2
        m = dict(base)
        m["x0"] = np.ascontiguousarray(x[b, h * TC:(h + 1) * TC, :])
        sel = np.zeros((128, 8), dtype=np.float32)
        if h == 1:
            sel[:, 2 * b] = 1.0
        m["selc"] = sel
        in_maps.append(m)

    res = run_bass_kernel_spmd(nc, in_maps, core_ids=list(range(NCORE)))
    full = np.empty((B, T, D), dtype=np.float32)
    for c in range(NCORE):
        b, h = c // 2, c % 2
        full[b, h * TC:(h + 1) * TC, :] = res.results[c]["out"]
    return full


# version 20
# speedup vs baseline: 1.0212x; 1.0212x over previous
import contextlib
import numpy as np

import concourse.bass as bass
import concourse.mybir as mybir
from concourse.bass_utils import run_bass_kernel_spmd

F32 = mybir.dt.float32
F32R = mybir.dt.float32r
BF16 = mybir.dt.bfloat16
F16 = mybir.dt.float16
FP8 = mybir.dt.float8e4
AF = mybir.ActivationFunctionType
ALU = mybir.AluOpType

B, T, D = 4, 4096, 1024
TC = 2048
NT = TC // 128
NCORE = 8
CHUNK = 128
NCH = TC // CHUNK
DST = 64
GC = 64
NGC = TC // GC
NH = 4
DK = 128
DV = 256
FF = 4096
MI = 2816
LN_EPS = 1e-5
RMS_EPS = 1e-6

N_STAGE = 4

_CACHE = {}


def _host_consts():
    n = 2 * CHUNK
    k = np.arange(n // 2, dtype=np.float64)
    t = np.arange(CHUNK, dtype=np.float64)
    ang = 2.0 * np.pi * np.outer(t, k) / n
    FcT = np.cos(ang)
    FsT = -np.sin(ang)
    Fny = np.cos(np.pi * t)[:, None]
    j = np.arange(CHUNK, dtype=np.float64)
    wgt = np.full(n // 2, 2.0 / n); wgt[0] = 1.0 / n
    angi = 2.0 * np.pi * np.outer(k, j) / n
    GcT = wgt[:, None] * np.cos(angi)
    GsT = -wgt[:, None] * np.sin(angi)
    Gny = (np.cos(np.pi * j) / n)[None, :]
    tri = (np.arange(GC)[:, None] <= np.arange(GC)[None, :]).astype(np.float64)
    return {
        "c_FcT": FcT.astype(np.float32), "c_FsT": FsT.astype(np.float32),
        "c_Fny": Fny.astype(np.float32), "c_GcT": GcT.astype(np.float32),
        "c_GsT": GsT.astype(np.float32), "c_Gny": Gny.astype(np.float32),
        "c_U": (tri / 16.0).astype(np.float32),
        "c_mask": tri.astype(np.float32),
        "c_id": np.eye(128, dtype=np.float32),
        "c_ones": np.ones((1, 128), dtype=np.float32),
        "c_eps": np.repeat(np.array([[1e-5, 1e-6]], dtype=np.float32), 128, axis=0),
        "c_U2b": np.kron(np.eye(2), -tri / 16.0).astype(np.float32),
        "c_mask2": np.kron(np.eye(2), tri).astype(np.float32),
        "c_ind0": np.repeat(np.array([[-1.0 / 16.0], [0.0]], dtype=np.float32), 64, axis=0),
        "c_ind1": np.repeat(np.array([[0.0], [-1.0 / 16.0]], dtype=np.float32), 64, axis=0),
    }


class Buf:
    __slots__ = ("name", "nslot", "w", "r")

    def __init__(self, name, nslot=1, seed=()):
        self.name = name
        self.nslot = nslot
        self.w = [list(seed) for _ in range(nslot)]
        self.r = [[] for _ in range(nslot)]

    def all(self):
        return [(self, i) for i in range(self.nslot)]

    def tokens(self):
        out = []
        for i in range(self.nslot):
            out.extend(self.w[i])
            out.extend(self.r[i])
        return out


class Prog:
    ENGS = ("t", "v", "s", "g", "y")

    def __init__(self, nc):
        self.nc = nc
        self.ops = {e: [] for e in self.ENGS}
        self.counts = {}
        self.handles = {}
        self.last_wait = {e: {} for e in self.ENGS}
        self._es = contextlib.ExitStack()
        for e in self.ENGS:
            self._sem("c" + e)
        self.dma_rr = {}
        for q in ("dy", "ds", "dg"):
            self.dma_rr[q] = 0
            for i in range(4):
                self._sem(q + str(i))
        self._sem("cc")

    def _sem(self, name):
        self.counts[name] = 0
        self.handles[name] = self._es.enter_context(self.nc.semaphore(name))

    @staticmethod
    def _norm(lst):
        out = []
        for x in lst or []:
            if x is None:
                continue
            if isinstance(x, Buf):
                out.append((x, 0))
            elif isinstance(x, tuple) and isinstance(x[0], Buf):
                out.append(x)
            elif isinstance(x, list):
                out.extend(Prog._norm(x))
            else:
                raise TypeError(f"bad dep {x}")
        return out

    def emit(self, eng, fn, reads=(), writes=(), extra_waits=(), dma=False,
             is_collective=False):
        reads = self._norm(reads)
        writes = self._norm(writes)
        waits = {}

        def add_tok(tok):
            if tok is None:
                return
            sn, v = tok
            if waits.get(sn, 0) < v:
                waits[sn] = v

        for (b, i) in reads:
            for tok in b.w[i]:
                add_tok(tok)
        for (b, i) in writes:
            for tok in b.w[i]:
                add_tok(tok)
            for tok in b.r[i]:
                add_tok(tok)
        for tok in extra_waits or []:
            add_tok(tok)

        if is_collective:
            semn, incby = "cc", 1
        elif dma:
            q = {"y": "dy", "s": "ds", "g": "dg"}[eng]
            ri = self.dma_rr[q]
            self.dma_rr[q] = (ri + 1) % 4
            semn = q + str(ri)
            incby = 16
            add_tok((semn, self.counts[semn]))
        else:
            semn, incby = "c" + eng, 1
        self.counts[semn] += incby
        tok = (semn, self.counts[semn])

        lw = self.last_wait[eng]
        eff = []
        for sn, v in waits.items():
            if lw.get(sn, 0) >= v:
                continue
            lw[sn] = v
            eff.append((sn, v))
        handles = self.handles

        def run(engine, eff=eff, fn=fn, semn=semn, incby=incby):
            for (sn, v) in eff:
                engine.wait_ge(handles[sn], v)
            instr = fn(engine)
            instr.then_inc(handles[semn], incby)
            return instr

        self.ops[eng].append(run)
        for (b, i) in reads:
            b.r[i].append(tok)
        for (b, i) in writes:
            b.w[i] = [tok]
            b.r[i] = []
        return tok

    def wait_only(self, eng, deps=(), extra_waits=()):
        return self.emit(eng, lambda e: e.nop(), reads=deps, extra_waits=extra_waits)

    def replay(self):
        block = self._es.enter_context(self.nc.Block())
        ops = self.ops

        @block.tensor
        def _(e):
            for f in ops["t"]: f(e)

        @block.vector
        def _(e):
            for f in ops["v"]: f(e)

        @block.scalar
        def _(e):
            for f in ops["s"]: f(e)

        @block.gpsimd
        def _(e):
            for f in ops["g"]: f(e)

        @block.sync
        def _(e):
            for f in ops["y"]: f(e)


def build_nc(n_stage):
    nc = bass.Bass("TRN2", target_bir_lowering=False)
    consts = _host_consts()

    dr = {}
    def din(name, shape):
        dr[name] = nc.dram_tensor(name, list(shape), F32, kind="ExternalInput").ap()
        return dr[name]

    din("x0", (TC, D))
    din("W_inT", (D, D)); din("b_in", (D,))
    din("gamma", (D,)); din("beta", (D,))
    din("A_re", (66,)); din("A_im", (66,))
    din("B_re", (66,)); din("B_im", (66,))
    din("CreT", (66, D)); din("CimT", (66, D))
    din("w1T", (D, FF)); din("b1", (FF,))
    din("w2T", (FF, D)); din("b2", (D,))
    din("attn_w", (D,))
    din("Wq", (D, 512)); din("Wk", (D, 512))
    din("Wv", (D, D)); din("Wg", (D, D))
    din("Wgk1", (D, 16)); din("Wgk2", (16, 512))
    din("bgk2", (512,)); din("gnw", (DV,))
    din("Wo", (D, D)); din("mlp_w", (D,))
    din("Wgate", (D, MI)); din("Wup", (D, MI))
    din("Wdown", (MI, D))
    din("selc", (128, 8))
    for cname, arr in consts.items():
        din(cname, arr.shape)

    out = nc.dram_tensor("out", [TC, D], F32, kind="ExternalOutput").ap()
    outT = out.rearrange("(tt p) d -> p tt d", p=128)

    ag1_in = nc.dram_tensor("ag1_in", [132, D], F32).ap()
    ag1_out = nc.dram_tensor("ag1_out", [NCORE * 132, D], F32).ap()
    ag2_in = nc.dram_tensor("ag2_in", [128, NH * DV], F32).ap()
    ag2_out = nc.dram_tensor("ag2_out", [NCORE * 128, NH * DV], F32).ap()
    spill = nc.dram_tensor("x2b_spill", [TC, D], BF16).ap()

    p = Prog(nc)
    es = p._es

    live = {}
    freed_tokens = []
    alloc_seq = [0]

    def sb(name, shape, dtype, nslot=1):
        cm = nc.sbuf_tensor("sb_" + name, list(shape), dtype)
        t = cm.__enter__()
        b = Buf(name, nslot, seed=tuple(freed_tokens))
        alloc_seq[0] += 1
        live[name] = (alloc_seq[0], cm, b)
        return t, b

    def sbfree(*names):
        for name in sorted(names, key=lambda n: -live[n][0]):
            _, cm, b = live.pop(name)
            freed_tokens.extend(b.tokens())
            cm.__exit__(None, None, None)
        best = {}
        for (sn, v) in freed_tokens:
            if best.get(sn, 0) < v:
                best[sn] = v
        freed_tokens[:] = list(best.items())

    def psum(name, shape, dtype=F32):
        t = es.enter_context(nc.psum_tensor("ps_" + name, list(shape), dtype))
        b = Buf(name, (shape[-1] * 4 + 2047) // 2048)
        return t, b

    def pb_(B, sl):
        return [(B, k) for k in range(sl.start // 512, (sl.stop + 511) // 512)]

    stream, Bstream = sb("stream", (128, NT, D), F32, nslot=NT)

    c_FcT, BcFcT = sb("c_FcT", (128, 128), F32R)
    c_FsT, BcFsT = sb("c_FsT", (128, 128), F32R)
    c_Fny, BcFny = sb("c_Fny", (128, 1), F32R)
    c_GcT, BcGcT = sb("c_GcT", (128, 128), F32R)
    c_GsT, BcGsT = sb("c_GsT", (128, 128), F32R)
    c_Gny, BcGny = sb("c_Gny", (1, 128), F32R)
    c_U, BcU = sb("c_U", (64, 64), F32R)
    c_mask, Bcmask = sb("c_mask", (64, 64), F32)
    c_id, Bcid = sb("c_id", (128, 128), F32)
    c_idb, Bcidb = sb("c_idb", (128, 128), BF16)
    c_ones, Bcones = sb("c_ones", (1, 128), F32R)
    c_eps, Bceps = sb("c_eps", (128, 2), F32)
    nc.const_aps.register(F32, LN_EPS, c_eps[:, 0:1])
    nc.const_aps.register(F32, RMS_EPS, c_eps[:, 1:2])

    bnscr, Bbn = sb("bnscr", (128, 2, 12), F32, nslot=2)
    mvt, Bmvt = sb("mvt", (128, 2, 2), F32, nslot=2)
    stat, Bstat = sb("stat", (128, 8 * NT), F32, nslot=NT)
    xh, Bxh = sb("xh", (128, 2, D), BF16, nslot=2)
    xhT, BxhT = sb("xhT", (128, 2, 8, 128), BF16, nslot=2)
    selcs, Bsel = sb("selcs", (128, 8), F32)

    pX, BpX = psum("pX", (128, 2048))
    pU, BpU = psum("pU", (128, 1024))
    pY, BpY = psum("pY", (128, 1024))
    pTb = pU[:].bitcast(BF16)

    def dma(queue, out_ap, in_ap, reads=(), writes=()):
        return p.emit(queue, lambda e: e.dma_start(out=out_ap, in_=in_ap),
                      reads=reads, writes=writes, dma=True)

    dma("y", stream[:], dr["x0"].rearrange("(tt p) d -> p tt d", p=128),
        writes=Bstream.all())
    for nm, tile, bb in (("c_FcT", c_FcT, BcFcT), ("c_FsT", c_FsT, BcFsT),
                         ("c_Fny", c_Fny, BcFny), ("c_GcT", c_GcT, BcGcT),
                         ("c_GsT", c_GsT, BcGsT), ("c_Gny", c_Gny, BcGny),
                         ("c_U", c_U, BcU), ("c_ones", c_ones, Bcones)):
        dma("g", tile[:], dr[nm], writes=[bb])
    for nm, tile, bb in (("c_mask", c_mask, Bcmask), ("c_id", c_id, Bcid),
                         ("c_eps", c_eps, Bceps)):
        dma("s", tile[:], dr[nm], writes=[bb])
    dma("g", c_idb[:], dr["c_id"], writes=[Bcidb])
    dma("s", selcs[:], dr["selc"], writes=[Bsel])

    def ln_stats_pair(i, col0, src_bufs, src0, src1, eps=LN_EPS):
        sl2 = i % 2
        bn = bnscr[:, sl2, :]
        mv = mvt[:, sl2, :]
        p.emit("v", lambda e, bn=bn: e.bn_stats(bn[:, 0:6], src0), reads=src_bufs,
               writes=[(Bbn, sl2)])
        p.emit("v", lambda e, bn=bn: e.bn_stats(bn[:, 6:12], src1), reads=src_bufs,
               writes=[(Bbn, sl2)])
        p.emit("v", lambda e, bn=bn, mv=mv: e.bn_aggr(mv, bn),
               reads=[(Bbn, sl2)], writes=[(Bmvt, sl2)])
        c = 8 * i + col0
        p.emit("s", lambda e, c=c, eps=eps, mv=mv: e.activation(
            out=stat[:, c:c + 1], in_=mv[:, 1:2], func=AF.Ln, bias=eps, scale=1.0),
            reads=[(Bmvt, sl2), Bceps], writes=[(Bstat, i)])
        p.emit("s", lambda e, c=c: e.activation(
            out=stat[:, c + 1:c + 2], in_=stat[:, c:c + 1], func=AF.Exp,
            bias=0.0, scale=-0.5),
            reads=[(Bstat, i)], writes=[(Bstat, i)])
        p.emit("v", lambda e, c=c, mv=mv: e.tensor_scalar(
            out=stat[:, c + 2:c + 3], in0=mv[:, 0:1],
            scalar1=stat[:, c + 1:c + 2], scalar2=-1.0, op0=ALU.mult, op1=ALU.mult),
            reads=[(Bmvt, sl2), (Bstat, i)], writes=[(Bstat, i)])

    def tp128(src_ap, src_bufs, dst_ap, dst_bufs, k, dtype="b"):
        bank = k % 2
        if dtype == "b":
            pdst = pTb[:, 1024 * bank:1024 * bank + 128]
        else:
            pdst = pU[:, 512 * bank:512 * bank + 128]
        ident = c_idb if dtype == "b" else c_id
        identb = Bcidb if dtype == "b" else Bcid
        p.emit("t", lambda e: e.matmul(pdst, src_ap, ident[:],
                                       start=True, stop=True, is_transpose=True),
               reads=list(src_bufs) + [identb], writes=[(BpU, bank)])
        p.emit("v", lambda e: e.tensor_copy(dst_ap, pdst),
               reads=[(BpU, bank)], writes=list(dst_bufs))

    def norm_transpose(src_ap, i, col0, src_bufs, slot):
        c = 8 * i + col0
        p.emit("s", lambda e, c=c, src_ap=src_ap, slot=slot: e.activation(
            out=xh[:, slot, :], in_=src_ap, func=AF.Identity,
            bias=stat[:, c + 2:c + 3], scale=stat[:, c + 1:c + 2]),
            reads=list(src_bufs) + [(Bstat, i)], writes=[(Bxh, slot)])
        for kdc in range(8):
            tp128(xh[:, slot, 128 * kdc:128 * (kdc + 1)], [(Bxh, slot)],
                  xhT[:, slot, kdc, :], [(BxhT, slot)], kdc)

    def ts(eng, out_ap, in_ap, s1, s2, op0, op1, rb, wb):
        p.emit(eng, lambda e: e.tensor_scalar(out=out_ap, in0=in_ap, scalar1=s1,
                                              scalar2=s2, op0=op0, op1=op1),
               reads=list(rb), writes=list(wb))

    def stt(eng, out_ap, in0_ap, scalar, in1_ap, op0, op1, rb, wb):
        p.emit(eng, lambda e: e.scalar_tensor_tensor(out=out_ap, in0=in0_ap,
                                                     scalar=scalar, in1=in1_ap,
                                                     op0=op0, op1=op1),
               reads=list(rb), writes=list(wb))

    def tt(eng, out_ap, in0_ap, in1_ap, op, rb, wb):
        p.emit(eng, lambda e: e.tensor_tensor(out=out_ap, in0=in0_ap, in1=in1_ap,
                                              op=op),
               reads=list(rb), writes=list(wb))

    evs, Bevs = sb("evs", (128, D), F32)
    tscr, Btscr = sb("tscr", (128, D), F32)
    tscr2, Btscr2 = sb("tscr2", (128, D), F32)
    aghold, Bagh = sb("aghold", (128, D), F32)
    abuf, Babuf = sb("abuf", (128, NCH, D), F32R, nslot=NCH)
    wWinT, BwWinT = sb("wWinT", (128, 8, D), BF16)
    brow, Bbrow = sb("brow", (1, D), F32R)
    gamma_b, Bgamma = sb("gamma_b", (128, D), F32)
    beta_b, Bbeta = sb("beta_b", (128, D), F32)
    growg, Bgrowg = sb("growg", (1, D), F32R)
    growb, Bgrowb = sb("growb", (1, D), F32R)

    dma("g", wWinT[:], dr["W_inT"].rearrange("(c p) d -> p c d", p=128),
        writes=[BwWinT])
    dma("g", brow[:], dr["b_in"][None, :], writes=[Bbrow])
    dma("g", growg[:], dr["gamma"][None, :], writes=[Bgrowg])
    dma("g", growb[:], dr["beta"][None, :], writes=[Bgrowb])

    for srcrow, bsrc, dst, bb in ((growg, Bgrowg, gamma_b, Bgamma),
                                  (growb, Bgrowb, beta_b, Bbeta)):
        for nh2 in range(2):
            p.emit("t", lambda e, srcrow=srcrow, nh2=nh2: e.matmul(
                pY[:, 0:512], c_ones[:], srcrow[:, 512 * nh2:512 * (nh2 + 1)],
                start=True, stop=True),
                reads=[bsrc, Bcones], writes=[(BpY, 0)])
            p.emit("v", lambda e, dst=dst, nh2=nh2: e.tensor_copy(
                dst[:, 512 * nh2:512 * (nh2 + 1)], pY[:, 0:512]),
                reads=[(BpY, 0)], writes=[(bb, 0)])

    for i in range(NT):
        xt = stream[:, i, :]
        ln_stats_pair(i, 0, [(Bstream, i)], xt[:, 0:512], xt[:, 512:1024])

    def a_prep(i):
        norm_transpose(stream[:, i, :], i, 0, [(Bstream, i)], i % 2)

    def a_mm(i):
        slot = i % 2
        for nh2 in range(2):
            sl = slice(512 * nh2, 512 * (nh2 + 1))
            for kdc in range(8):
                p.emit("t", lambda e, kdc=kdc, sl=sl, slot=slot: e.matmul(
                    pY[:, sl], xhT[:, slot, kdc, :], wWinT[:, kdc, sl],
                    start=(kdc == 0), stop=False),
                    reads=[(BxhT, slot), BwWinT],
                    writes=[(BpY, nh2)] if kdc == 0 else [])
            p.emit("t", lambda e, sl=sl, nh2=nh2: e.matmul(
                pY[:, sl], c_ones[:], brow[:, sl], start=False, stop=True),
                reads=[Bcones, Bbrow], writes=[(BpY, nh2)])
        ln_stats_pair(i, 3, [(BpY, 0), (BpY, 1)], pY[:, 0:512], pY[:, 512:1024])

    def a_evac(i):
        c = 8 * i + 3
        for nh2 in range(2):
            sl = slice(512 * nh2, 512 * (nh2 + 1))
            p.emit("s", lambda e, c=c, sl=sl, nh2=nh2: e.activation(
                out=evs[:, sl], in_=pY[:, sl], func=AF.Identity,
                bias=stat[:, c + 2:c + 3], scale=stat[:, c + 1:c + 2]),
                reads=[(BpY, nh2), (Bstat, i)], writes=[Bevs])
        tt("v", evs[:], evs[:], gamma_b[:], ALU.mult, [Bevs, Bgamma], [Bevs])
        p.emit("g", lambda e, i=i: e.tensor_tensor(
            out=abuf[:, i, :], in0=evs[:], in1=beta_b[:], op=ALU.add),
            reads=[Bevs, Bbeta], writes=[(Babuf, i)])

    stages = [a_prep, a_mm, a_evac]
    for step in range(NT + len(stages) - 1):
        for si in range(len(stages) - 1, -1, -1):
            i = step - si
            if 0 <= i < NT:
                stages[si](i)

    sbfree("wWinT", "brow", "gamma_b", "beta_b", "growg", "growb")

    if n_stage == 0:
        for i in range(NT):
            dma("y", outT[:, i, :], abuf[:, i, :].bitcast(F32), reads=[(Babuf, i)])
        p.wait_only("y", extra_waits=[("dy" + str(i), p.counts["dy" + str(i)])
                                      for i in range(4)])
        p.replay()
        return nc

    NS = 66
    Kfre, BKfre = sb("Kfre", (128, D), F32)
    Kfim, BKfim = sb("Kfim", (128, D), F32)
    Kfny, BKfny = sb("Kfny", (1, D), F32)
    PTre, BPTre = sb("PTre", (128, NS), F32R)
    PTim, BPTim = sb("PTim", (128, NS), F32R)
    QTre, BQTre = sb("QTre", (NS, 128), F32R)
    QTimN, BQTimN = sb("QTimN", (NS, 128), F32R)
    accol, Baccol = sb("accol", (NS, 3), F32)
    wcre, Bwcre = sb("wcre", (NS, D), F32)
    wcim, Bwcim = sb("wcim", (NS, D), F32)
    Sre, BSre = sb("Sre", (NS, D), F32)
    Sim, BSim = sb("Sim", (NS, D), F32)
    Gre, BGre = sb("Gre", (NS, D), F32R)
    Gim, BGim = sb("Gim", (NS, D), F32R)
    PreS, BPreS = sb("PreS", (128, D), F32R)
    PimS, BPimS = sb("PimS", (128, D), F32R)
    pnyr, Bpnyr = sb("pnyr", (1, D), F32R)
    acol, Bacol = sb("acol", (NS, 4), F32)
    mcol, Bmcol = sb("mcol", (NS, 8), F32)
    negb, Bnegb = sb("negb", (NS, 2), F32)
    pwre, Bpwre = Kfre[0:NS, 0:256], BKfre
    pwim, Bpwim = Kfim[0:NS, 0:256], BKfim
    pvre, Bpvre = Kfre[0:NS, 256:384], BKfre
    pvim, Bpvim = Kfim[0:NS, 256:384], BKfim
    pbre, Bpbre = sb("pbre", (NS, 128), F32R)
    pbimN, BpbimN = sb("pbimN", (NS, 128), F32R)
    CreTr, BCreTr = Gre[:], BGre
    CimTr, BCimTr = Gim[:], BGim
    Krs, BKrs = PreS[:], BPreS

    dma("s", acol[:, 0:1], dr["A_re"][:, None], writes=[Bacol])
    dma("s", acol[:, 1:2], dr["A_im"][:, None], writes=[Bacol])
    dma("s", acol[:, 2:3], dr["B_re"][:, None], writes=[Bacol])
    dma("s", acol[:, 3:4], dr["B_im"][:, None], writes=[Bacol])
    dma("g", CreTr, dr["CreT"], writes=[BCreTr])
    dma("g", CimTr, dr["CimT"], writes=[BCimTr])

    p.emit("g", lambda e: e.memset(pwre[:, 0:1], 1.0), writes=[Bpwre])
    p.emit("g", lambda e: e.memset(pwim[:, 0:1], 0.0), writes=[Bpwim])
    p.emit("g", lambda e: e.memset(pvre[:, 127:128], 1.0), writes=[Bpvre])
    p.emit("g", lambda e: e.memset(pvim[:, 127:128], 0.0), writes=[Bpvim])
    p.emit("v", lambda e: e.tensor_copy(mcol[:, 0:2], acol[:, 0:2]),
           reads=[Bacol], writes=[Bmcol])
    ts("v", mcol[:, 2:3], acol[:, 1:2], -1.0, None, ALU.mult, ALU.bypass,
       [Bacol], [Bmcol])
    cur = 0
    for k in range(8):
        w = 1 << k
        mre = mcol[:, 3 * cur + 0:3 * cur + 1]
        mim = mcol[:, 3 * cur + 1:3 * cur + 2]
        mnim = mcol[:, 3 * cur + 2:3 * cur + 3]
        ts("v", tscr[0:NS, 0:w], pwre[:, 0:w], mre, None, ALU.mult, ALU.bypass,
           [Bpwre, Bmcol], [Btscr])
        stt("v", pwre[:, w:2 * w], pwim[:, 0:w], mnim, tscr[0:NS, 0:w],
            ALU.mult, ALU.add, [Bpwim, Bmcol, Btscr], [Bpwre])
        ts("v", tscr[0:NS, 0:w], pwre[:, 0:w], mim, None, ALU.mult, ALU.bypass,
           [Bpwre, Bmcol], [Btscr])
        stt("v", pwim[:, w:2 * w], pwim[:, 0:w], mre, tscr[0:NS, 0:w],
            ALU.mult, ALU.add, [Bpwim, Bmcol, Btscr], [Bpwim])
        if w < 128:
            lo, hi = 128 - 2 * w, 128 - w
            ts("v", tscr[0:NS, 0:w], pvre[:, hi:128], mre, None, ALU.mult,
               ALU.bypass, [Bpvre, Bmcol], [Btscr])
            stt("v", pvre[:, lo:hi], pvim[:, hi:128], mnim, tscr[0:NS, 0:w],
                ALU.mult, ALU.add, [Bpvim, Bmcol, Btscr], [Bpvre])
            ts("v", tscr[0:NS, 0:w], pvre[:, hi:128], mim, None, ALU.mult,
               ALU.bypass, [Bpvre, Bmcol], [Btscr])
            stt("v", pvim[:, lo:hi], pvim[:, hi:128], mre, tscr[0:NS, 0:w],
                ALU.mult, ALU.add, [Bpvim, Bmcol, Btscr], [Bpvim])
        if k < 7:
            nxt = 1 - cur
            nre = mcol[:, 3 * nxt + 0:3 * nxt + 1]
            nim = mcol[:, 3 * nxt + 1:3 * nxt + 2]
            nnim = mcol[:, 3 * nxt + 2:3 * nxt + 3]
            ts("v", mcol[:, 6:7], mre, mre, None, ALU.mult, ALU.bypass,
               [Bmcol], [Bmcol])
            stt("v", nre, mim, mnim, mcol[:, 6:7], ALU.mult, ALU.add,
                [Bmcol], [Bmcol])
            ts("v", nim, mre, mim, 2.0, ALU.mult, ALU.mult, [Bmcol], [Bmcol])
            ts("v", nnim, nim, -1.0, None, ALU.mult, ALU.bypass, [Bmcol], [Bmcol])
            cur = nxt

    ts("v", negb[:, 0:1], acol[:, 3:4], -1.0, None, ALU.mult, ALU.bypass,
       [Bacol], [Bnegb])
    ts("v", negb[:, 1:2], acol[:, 2:3], -1.0, None, ALU.mult, ALU.bypass,
       [Bacol], [Bnegb])
    ts("v", tscr[0:NS, 0:128], pwre[:, 0:128], acol[:, 2:3], None, ALU.mult,
       ALU.bypass, [Bpwre, Bacol], [Btscr])
    stt("v", pbre[:], pwim[:, 0:128], negb[:, 0:1], tscr[0:NS, 0:128],
        ALU.mult, ALU.add, [Bpwim, Bnegb, Btscr], [Bpbre])
    ts("v", tscr[0:NS, 0:128], pwre[:, 0:128], negb[:, 0:1], None, ALU.mult,
       ALU.bypass, [Bpwre, Bnegb], [Btscr])
    stt("v", pbimN[:], pwim[:, 0:128], negb[:, 1:2], tscr[0:NS, 0:128],
        ALU.mult, ALU.add, [Bpwim, Bnegb, Btscr], [BpbimN])

    ts("v", tscr[0:NS, :], CreTr.bitcast(F32), acol[:, 2:3], None, ALU.mult,
       ALU.bypass, [BCreTr, Bacol], [Btscr])
    stt("v", wcre[:], CimTr.bitcast(F32), negb[:, 0:1], tscr[0:NS, :],
        ALU.mult, ALU.add, [BCimTr, Bnegb, Btscr], [Bwcre])
    ts("v", tscr[0:NS, :], CreTr.bitcast(F32), acol[:, 3:4], None, ALU.mult,
       ALU.bypass, [BCreTr, Bacol], [Btscr])
    stt("v", wcim[:], CimTr.bitcast(F32), acol[:, 2:3], tscr[0:NS, :],
        ALU.mult, ALU.add, [BCimTr, Bacol, Btscr], [Bwcim])

    for src, sbf, dst, dbf in ((pvre, Bpvre, PTre, BPTre), (pvim, Bpvim, PTim, BPTim)):
        p.emit("t", lambda e, src=src: e.matmul(
            pU[:, 0:NS], src[:], c_id[0:NS, 0:NS],
            start=True, stop=True, is_transpose=True),
            reads=[sbf, Bcid], writes=[(BpU, 0)])
        p.emit("s", lambda e, dst=dst: e.activation(
            out=dst[:], in_=pU[:, 0:NS], func=AF.Identity,
            bias=0.0, scale=1.0),
            reads=[(BpU, 0)], writes=[dbf])
    p.emit("s", lambda e: e.activation(out=QTre[:], in_=pwre[:, 1:129],
                                       func=AF.Identity, bias=0.0, scale=1.0),
           reads=[Bpwre], writes=[BQTre])
    ts("v", QTimN[:], pwim[:, 1:129], -1.0, None, ALU.mult, ALU.bypass,
       [Bpwim], [BQTimN])
    p.emit("v", lambda e: e.tensor_copy(accol[:, 0:1], pwre[:, 128:129]),
           reads=[Bpwre], writes=[Baccol])
    p.emit("v", lambda e: e.tensor_copy(accol[:, 1:2], pwim[:, 128:129]),
           reads=[Bpwim], writes=[Baccol])
    ts("v", accol[:, 2:3], pwim[:, 128:129], -1.0, None, ALU.mult, ALU.bypass,
       [Bpwim], [Baccol])


    for nh2 in range(2):
        sl = slice(512 * nh2, 512 * (nh2 + 1))
        p.emit("t", lambda e, sl=sl: e.matmul(pY[:, sl], pbre[:], CreTr[:, sl],
                                              start=True, stop=False),
               reads=[Bpbre, BCreTr], writes=[(BpY, nh2)])
        p.emit("t", lambda e, sl=sl: e.matmul(pY[:, sl], pbimN[:], CimTr[:, sl],
                                              start=False, stop=True),
               reads=[BpbimN, BCimTr], writes=[(BpY, nh2)])
    p.emit("s", lambda e: e.activation(out=Krs[:], in_=pY[:], func=AF.Identity,
                                       bias=0.0, scale=1.0),
           reads=[(BpY, 0), (BpY, 1)], writes=[BKrs])
    for comp, lhs, lb, dstb, dst in ((0, c_FcT, BcFcT, BKfre, Kfre),
                                     (1, c_FsT, BcFsT, BKfim, Kfim)):
        for nh2 in range(2):
            sl = slice(512 * nh2, 512 * (nh2 + 1))
            po = slice(1024 * comp + 512 * nh2, 1024 * comp + 512 * (nh2 + 1))
            p.emit("t", lambda e, lhs=lhs, sl=sl, po=po: e.matmul(
                pX[:, po], lhs[:], Krs[:, sl], start=True, stop=True),
                reads=[lb, BKrs], writes=pb_(BpX, po))
        p.emit("s", lambda e, comp=comp, dst=dst: e.activation(
            out=dst[:], in_=pX[:, 1024 * comp:1024 * (comp + 1)],
            func=AF.Identity, bias=0.0, scale=1.0),
            reads=[(BpX, 2 * comp), (BpX, 2 * comp + 1)], writes=[dstb])
    for nh2 in range(2):
        sl = slice(512 * nh2, 512 * (nh2 + 1))
        p.emit("t", lambda e, sl=sl, nh2=nh2: e.matmul(pU[0:1, sl], c_Fny[:],
                                              Krs[:, sl], start=True, stop=True),
               reads=[BcFny, BKrs], writes=[(BpU, nh2)])
    p.emit("v", lambda e: e.tensor_copy(Kfny[:], pU[0:1, :]),
           reads=[(BpU, 0), (BpU, 1)], writes=[BKfny])
    sbfree("acol", "mcol", "negb", "pbre", "pbimN")

    p.emit("g", lambda e: e.memset(Sre[:], 0.0), writes=[BSre])
    p.emit("g", lambda e: e.memset(Sim[:], 0.0), writes=[BSim])

    def U_mms(i):
        for comp, lhs, lb in ((0, PTre, BPTre), (1, PTim, BPTim)):
            for nh2 in range(2):
                sl = slice(512 * nh2, 512 * (nh2 + 1))
                po = slice(1024 * comp + 512 * nh2, 1024 * comp + 512 * (nh2 + 1))
                p.emit("t", lambda e, lhs=lhs, sl=sl, po=po, i=i: e.matmul(
                    pX[0:NS, po], lhs[:], abuf[:, i, sl], start=True, stop=True),
                    reads=[lb, (Babuf, i)], writes=pb_(BpX, po))

    def S_update():
        ts("v", tscr[0:NS, :], Sre[:], accol[:, 0:1], None, ALU.mult, ALU.bypass,
           [BSre, Baccol], [Btscr])
        stt("v", tscr[0:NS, :], Sim[:], accol[:, 2:3], tscr[0:NS, :],
            ALU.mult, ALU.add, [BSim, Baccol, Btscr], [Btscr])
        ts("v", tscr2[0:NS, :], Sim[:], accol[:, 0:1], None, ALU.mult, ALU.bypass,
           [BSim, Baccol], [Btscr2])
        stt("v", tscr2[0:NS, :], Sre[:], accol[:, 1:2], tscr2[0:NS, :],
            ALU.mult, ALU.add, [BSre, Baccol, Btscr2], [Btscr2])
        tt("v", Sre[:], tscr[0:NS, :], pX[0:NS, 0:1024], ALU.add,
           [Btscr, (BpX, 0), (BpX, 1)], [BSre])
        tt("v", Sim[:], tscr2[0:NS, :], pX[0:NS, 1024:2048], ALU.add,
           [Btscr2, (BpX, 2), (BpX, 3)], [BSim])

    for i in range(NCH):
        U_mms(i)
        S_update()

    dma("g", ag1_in[0:NS, :], Sre[:], reads=[BSre])
    dma("g", ag1_in[NS:2 * NS, :], Sim[:], reads=[BSim])
    cc_tok = p.emit("g", lambda e: e.collective_compute(
        "AllGather", ALU.bypass, ins=[ag1_in.opt()], outs=[ag1_out.opt()],
        replica_groups=[list(range(NCORE))]),
        extra_waits=[("dg" + str(i), p.counts["dg" + str(i)]) for i in range(4)],
        is_collective=True)
    p.emit("g", lambda e: e.memset(Sre[:], 0.0), writes=[BSre])
    p.emit("g", lambda e: e.memset(Sim[:], 0.0), writes=[BSim])
    for r in range(NCORE):
        p.emit("g", lambda e, r=r: e.dma_start(
            out=aghold[0:NS, :], in_=ag1_out[2 * NS * r:2 * NS * r + NS, :]),
            writes=[Bagh], extra_waits=[cc_tok], dma=True)
        p.emit("g", lambda e, r=r: e.dma_start(
            out=evs[0:NS, :], in_=ag1_out[2 * NS * r + NS:2 * NS * (r + 1), :]),
            writes=[Bevs], extra_waits=[cc_tok], dma=True)
        stt("v", Sre[:], aghold[0:NS, :], selcs[0:NS, r:r + 1], Sre[:],
            ALU.mult, ALU.add, [Bagh, Bsel, BSre], [BSre])
        stt("v", Sim[:], evs[0:NS, :], selcs[0:NS, r:r + 1], Sim[:],
            ALU.mult, ALU.add, [Bevs, Bsel, BSim], [BSim])

    xnyr = aghold[0:1, :]
    Bxnyr = Bagh

    for i in range(NCH):
        tt("g", tscr[0:NS, :], wcre[:], Sre[:], ALU.mult, [Bwcre, BSre], [Btscr])
        tt("g", tscr2[0:NS, :], wcim[:], Sim[:], ALU.mult, [Bwcim, BSim], [Btscr2])
        tt("g", Gre[:], tscr[0:NS, :], tscr2[0:NS, :], ALU.subtract,
           [Btscr, Btscr2], [BGre])
        tt("g", tscr[0:NS, :], wcre[:], Sim[:], ALU.mult, [Bwcre, BSim], [Btscr])
        tt("g", tscr2[0:NS, :], wcim[:], Sre[:], ALU.mult, [Bwcim, BSre], [Btscr2])
        tt("g", Gim[:], tscr[0:NS, :], tscr2[0:NS, :], ALU.add,
           [Btscr, Btscr2], [BGim])
        for nh2 in range(2):
            sl = slice(512 * nh2, 512 * (nh2 + 1))
            p.emit("t", lambda e, sl=sl, i=i, nh2=nh2: e.matmul(
                pU[0:1, sl], c_Fny[:], abuf[:, i, sl],
                start=True, stop=True),
                reads=[BcFny, (Babuf, i)], writes=[(BpU, nh2)])
        p.emit("v", lambda e: e.tensor_copy(xnyr, pU[0:1, :]),
               reads=[(BpU, 0), (BpU, 1)], writes=[Bxnyr])
        for comp, lhs, lb in ((0, c_FcT, BcFcT), (1, c_FsT, BcFsT)):
            for nh2 in range(2):
                sl = slice(512 * nh2, 512 * (nh2 + 1))
                po = slice(1024 * comp + 512 * nh2, 1024 * comp + 512 * (nh2 + 1))
                p.emit("t", lambda e, lhs=lhs, sl=sl, po=po, i=i: e.matmul(
                    pX[:, po], lhs[:], abuf[:, i, sl], start=True, stop=True),
                    reads=[lb, (Babuf, i)], writes=pb_(BpX, po))
        tt("v", tscr[:], pX[:, 0:1024], Kfre[:], ALU.mult,
           [(BpX, 0), (BpX, 1), BKfre], [Btscr])
        tt("v", tscr2[:], pX[:, 1024:2048], Kfim[:], ALU.mult,
           [(BpX, 2), (BpX, 3), BKfim], [Btscr2])
        tt("g", PreS[:], tscr[:], tscr2[:], ALU.subtract, [Btscr, Btscr2], [BPreS])
        tt("v", tscr[:], pX[:, 0:1024], Kfim[:], ALU.mult,
           [(BpX, 0), (BpX, 1), BKfim], [Btscr])
        tt("v", tscr2[:], pX[:, 1024:2048], Kfre[:], ALU.mult,
           [(BpX, 2), (BpX, 3), BKfre], [Btscr2])
        tt("g", PimS[:], tscr[:], tscr2[:], ALU.add, [Btscr, Btscr2], [BPimS])
        tt("v", pnyr[:], xnyr, Kfny[:], ALU.mult, [Bxnyr, BKfny], [Bpnyr])
        for nh2 in range(2):
            sl = slice(512 * nh2, 512 * (nh2 + 1))
            p.emit("t", lambda e, sl=sl: e.matmul(pY[:, sl], c_GcT[:], PreS[:, sl],
                                                  start=True, stop=False),
                   reads=[BcGcT, BPreS], writes=[(BpY, nh2)])
            p.emit("t", lambda e, sl=sl: e.matmul(pY[:, sl], c_GsT[:], PimS[:, sl],
                                                  start=False, stop=False),
                   reads=[BcGsT, BPimS])
            p.emit("t", lambda e, sl=sl: e.matmul(pY[:, sl], QTre[:], Gre[:, sl],
                                                  start=False, stop=False),
                   reads=[BQTre, BGre])
            p.emit("t", lambda e, sl=sl: e.matmul(pY[:, sl], QTimN[:], Gim[:, sl],
                                                  start=False, stop=False),
                   reads=[BQTimN, BGim])
            p.emit("t", lambda e, sl=sl, nh2=nh2: e.matmul(
                pY[:, sl], c_Gny[:], pnyr[:, sl], start=False, stop=True),
                reads=[BcGny, Bpnyr], writes=[(BpY, nh2)])
        U_mms(i)
        for nh2 in range(2):
            sl = slice(512 * nh2, 512 * (nh2 + 1))
            stt("v", evs[:, sl], pY[:, sl], 1.0, abuf[:, i, sl].bitcast(F32),
                ALU.mult, ALU.add, [(BpY, nh2), (Babuf, i)], [Bevs])
        p.emit("g", lambda e, i=i: e.tensor_tensor(
            out=stream[:, i, :], in0=stream[:, i, :], in1=evs[:], op=ALU.add),
            reads=[Bevs, (Bstream, i)], writes=[(Bstream, i)])
        S_update()

    sbfree("pnyr", "Kfre", "Kfim", "Kfny", "PTre", "PTim", "QTre", "QTimN",
           "accol", "wcre", "wcim", "Sre", "Sim", "Gre", "Gim", "PreS", "PimS",
           "abuf", "evs", "tscr", "tscr2", "aghold")



    def ffn_stage(tag, pre_w2=None):
        w1bf, Bw1 = sb("w1bf" + tag, (128, 8, FF), BF16, nslot=8)
        if pre_w2 is None:
            w2q, Bw2 = sb("w2q" + tag, (128, 32, D), FP8, nslot=4)
        else:
            w2q, Bw2 = pre_w2
        hqT, Bhq = sb("hqT" + tag, (128, 32, 512), FP8, nslot=32)
        xfT, Bxf = sb("xfT" + tag, (128, 8, 512), BF16, nslot=4)
        b1c, Bb1c = sb("b1c" + tag, (128, 32), F32)
        b1row, Bb1row = sb("b1row" + tag, (32, 128), F32)
        b2q, Bb2q = sb("b2q" + tag, (1, D), F16)
        b2s, Bb2s = sb("b2s" + tag, (1, D), F32)
        onesh, Bonesh = sb("onesh" + tag, (1, 128), F16)
        evs16, Bevs16 = sb("evs16" + tag, (128, 2, D), F16, nslot=2)

        w1r = dr["w1T"].rearrange("(c p) h -> p c h", p=128)
        for kdc in range(8):
            dma("g", w1bf[:, kdc, :], w1r[:, kdc, :], writes=[(Bw1, kdc)])
        dma("s", b1row[:], dr["b1"].rearrange("(hh p) -> hh p", p=128),
            writes=[Bb1row])
        p.emit("t", lambda e: e.matmul(
            pU[:, 0:32], b1row[:], c_id[0:32, 0:32],
            start=True, stop=True, is_transpose=True),
            reads=[Bb1row, Bcid], writes=[(BpU, 0)])
        p.emit("v", lambda e: e.tensor_copy(b1c[:], pU[:, 0:32]),
               reads=[(BpU, 0)], writes=[Bb1c])
        dma("s", b2s[:], dr["b2"][None, :], writes=[Bb2s])
        dma("g", onesh[:], dr["c_ones"], writes=[Bonesh])
        p.emit("s", lambda e: e.activation(out=b2q[:], in_=b2s[:],
                                           func=AF.Identity, bias=0.0, scale=1.0),
               reads=[Bb2s], writes=[Bb2q])
        if pre_w2 is None:
            w2r = dr["w2T"].rearrange("(c p) d -> p c d", p=128)
            for jq in range(4):
                dma("g", w2q[:, 8 * jq:8 * (jq + 1), :],
                    w2r[:, 8 * jq:8 * (jq + 1), :], writes=[(Bw2, jq)])

        def c_prep(g4):
            tiles = list(range(4 * g4, 4 * g4 + 4))
            for li, i in enumerate(tiles):
                xt = stream[:, i, :]
                ln_stats_pair(i, 0, [(Bstream, i)], xt[:, 0:512], xt[:, 512:1024])
                slot = li % 2
                c = 8 * i
                p.emit("s", lambda e, c=c, xt=xt, slot=slot: e.activation(
                    out=xh[:, slot, :], in_=xt, func=AF.Identity,
                    bias=stat[:, c + 2:c + 3], scale=stat[:, c + 1:c + 2]),
                    reads=[(Bstream, i), (Bstat, i)], writes=[(Bxh, slot)])
                for kdc in range(8):
                    tp128(xh[:, slot, 128 * kdc:128 * (kdc + 1)], [(Bxh, slot)],
                          xfT[:, kdc, 128 * li:128 * (li + 1)], [(Bxf, li)], kdc)

        def c_ffn1(g4):
            for hh in range(32):
                po = slice(512 * (hh % 4), 512 * (hh % 4) + 512)
                for kdc in range(8):
                    p.emit("t", lambda e, hh=hh, kdc=kdc, po=po: e.matmul(
                        pX[:, po], w1bf[:, kdc, 128 * hh:128 * (hh + 1)],
                        xfT[:, kdc, :], start=(kdc == 0), stop=(kdc == 7)),
                        reads=[(Bw1, kdc)] + Bxf.all(),
                        writes=pb_(BpX, po) if kdc in (0, 7) else [])
                p.emit("s", lambda e, hh=hh, po=po: e.activation(
                    out=hqT[:, hh, :], in_=pX[:, po], func=AF.Relu,
                    bias=b1c[:, hh:hh + 1], scale=1.0),
                    reads=pb_(BpX, po) + [Bb1c], writes=[(Bhq, hh)])

        def c_ffn2(g4):
            tiles = list(range(4 * g4, 4 * g4 + 4))
            for li, i in enumerate(tiles):
                eslot = li % 2
                for nh2 in range(2):
                    sl = slice(512 * nh2, 512 * (nh2 + 1))
                    for hh in range(0, 32, 2):
                        p.emit("t", lambda e, hh=hh, li=li, sl=sl: e.matmul(
                            pY[:, sl],
                            hqT[:, hh:hh + 2, 128 * li:128 * (li + 1)],
                            w2q[:, hh:hh + 2, sl], start=(hh == 0), stop=False,
                            perf_mode=mybir.MatmulPerfMode.DoubleRow),
                            reads=[(Bhq, hh), (Bhq, hh + 1), (Bw2, hh // 8)],
                            writes=[(BpY, nh2)] if hh == 0 else [])
                    p.emit("t", lambda e, sl=sl, nh2=nh2: e.matmul(
                        pY[:, sl], onesh[:], b2q[:, sl], start=False, stop=True),
                        reads=[Bonesh, Bb2q], writes=[(BpY, nh2)])
                    p.emit("s", lambda e, sl=sl, eslot=eslot, nh2=nh2: e.activation(
                        out=evs16[:, eslot, sl], in_=pY[:, sl], func=AF.Identity,
                        bias=0.0, scale=1.0),
                        reads=[(BpY, nh2)], writes=[(Bevs16, eslot)])
                p.emit("g", lambda e, i=i, eslot=eslot: e.tensor_tensor(
                    out=stream[:, i, :], in0=stream[:, i, :],
                    in1=evs16[:, eslot, :], op=ALU.add),
                    reads=[(Bevs16, eslot), (Bstream, i)], writes=[(Bstream, i)])

        c_prep(0)
        c_ffn1(0)
        for g4 in range(1, 4):
            c_prep(g4)
            c_ffn2(g4 - 1)
            c_ffn1(g4)
        c_ffn2(3)

        sbfree("w1bf" + tag, "w2q" + tag, "hqT" + tag,
               "xfT" + tag, "b1c" + tag, "b1row" + tag, "b2q" + tag,
               "b2s" + tag, "onesh" + tag, "evs16" + tag)

    if n_stage >= 2:
        ffn_stage("c")

    if n_stage >= 3:
        QESC = float(DK) ** -0.5
        c_U2b, BcU2b = sb("c_U2b", (128, 128), F32R)
        c_mask2, Bcm2 = sb("c_mask2", (128, 128), F32)
        c_ind0, Bci0 = sb("c_ind0", (128, 1), F32R)
        c_ind1, Bci1 = sb("c_ind1", (128, 1), F32R)
        dma("g", c_U2b[:], dr["c_U2b"], writes=[BcU2b])
        dma("s", c_mask2[:], dr["c_mask2"], writes=[Bcm2])
        dma("g", c_ind0[:], dr["c_ind0"], writes=[Bci0])
        dma("g", c_ind1[:], dr["c_ind1"], writes=[Bci1])

        awc, Bawc = sb("awc", (128, 8), F32)
        awcrow, Bawcrow = sb("awcrow", (8, 128), F32)
        dma("s", awcrow[:], dr["attn_w"].rearrange("(c p) -> c p", p=128),
            writes=[Bawcrow])
        p.emit("t", lambda e: e.matmul(
            pU[:, 0:8], awcrow[:], c_id[0:8, 0:8],
            start=True, stop=True, is_transpose=True),
            reads=[Bawcrow, Bcid], writes=[(BpU, 0)])
        p.emit("v", lambda e: e.tensor_copy(awc[:], pU[:, 0:8]),
               reads=[(BpU, 0)], writes=[Bawc])

        BpUu = Buf("pUu", 2)
        kexpS, BkexpS = sb("kexpS", (128, NT, 512), BF16, nslot=NT)
        vS, BvS = sb("vS", (128, NT, D), BF16, nslot=NT)
        ebc, Bebc = sb("ebc", (128, NT, 8), F32, nslot=NT)
        Sgla, BSgla = sb("Sgla", (128, NH, DV), F32, nslot=NH)
        Sb16, BSb16 = sb("Sb16", (128, NH + 2, DV), BF16, nslot=NH + 2)
        rcrow, Brcrow = sb("rcrow", (1, NT, 128), BF16, nslot=NT)
        agh2, Bagh2 = sb("agh2", (128, D), F32)
        sigz, Bsigz = sb("sigz", (128, 512), F32, nslot=2)
        tscrD, BtscrD = sigz, Bsigz
        glcur, Bglcur = sb("glcur", (128, 512), F32R)
        expsc, Bexpsc = sb("expsc", (128, 512), F32)
        z1s, Bz1s = sb("z1s", (128, 16), BF16)
        z1T, Bz1T = sb("z1T", (16, 128), BF16)
        ebrow, Bebrow = sb("ebrow", (1, 512), F32)
        gcol, Bgcol = sb("gcol", (128, 2), F32)

        Wgk1b, BWgk1b = sb("Wgk1b", (128, 8, 16), BF16)
        Wgk2b, BWgk2b = sb("Wgk2b", (16, 512), BF16)
        bgk2r, Bbgk2r = sb("bgk2r", (1, 512), BF16)
        gnwc, Bgnwc = sb("gnwc", (128, 2), F32)
        gnwcrow, Bgnwcrow = sb("gnwcrow", (2, 128), F32)
        dma("s", gnwcrow[:], dr["gnw"].rearrange("(c p) -> c p", p=128),
            writes=[Bgnwcrow])
        p.emit("t", lambda e: e.matmul(
            pU[:, 0:2], gnwcrow[:], c_id[0:2, 0:2],
            start=True, stop=True, is_transpose=True),
            reads=[Bgnwcrow, Bcid], writes=[(BpU, 0)])
        p.emit("v", lambda e: e.tensor_copy(gnwc[:], pU[:, 0:2]),
               reads=[(BpU, 0)], writes=[Bgnwc])
        WqB, BWqB = sb("WqB", (128, 8, 512), BF16, nslot=8)
        WqBr = dr["Wq"].rearrange("(c p) o -> p c o", p=128)
        for kdc in range(8):
            dma("g", WqB[:, kdc, :], WqBr[:, kdc, :], writes=[(BWqB, kdc)])
        for kdc in range(8):
            ts("v", WqB[:, kdc, :], WqB[:, kdc, :], awc[:, kdc:kdc + 1], None,
               ALU.mult, ALU.bypass, [(BWqB, kdc), Bawc], [(BWqB, kdc)])

        WkB, BWkB = sb("WkB", (128, 8, 512), BF16, nslot=8)
        WvB, BWvB = sb("WvB", (128, 8, D), BF16, nslot=8)
        dma("g", Wgk1b[:], dr["Wgk1"].rearrange("(c p) o -> p c o", p=128),
            writes=[BWgk1b])
        dma("g", Wgk2b[:], dr["Wgk2"], writes=[BWgk2b])
        dma("g", bgk2r[:], dr["bgk2"][None, :], writes=[Bbgk2r])
        WkBr = dr["Wk"].rearrange("(c p) o -> p c o", p=128)
        for kdc in range(8):
            dma("g", WkB[:, kdc, :], WkBr[:, kdc, :], writes=[(BWkB, kdc)])
        WvBr = dr["Wv"].rearrange("(c p) o -> p c o", p=128)
        for kdc in range(8):
            dma("g", WvB[:, kdc, :], WvBr[:, kdc, :], writes=[(BWvB, kdc)])
        for kdc in range(8):
            ts("v", WkB[:, kdc, :], WkB[:, kdc, :], awc[:, kdc:kdc + 1], None,
               ALU.mult, ALU.bypass, [(BWkB, kdc), Bawc], [(BWkB, kdc)])
            ts("v", WvB[:, kdc, :], WvB[:, kdc, :], awc[:, kdc:kdc + 1], None,
               ALU.mult, ALU.bypass, [(BWvB, kdc), Bawc], [(BWvB, kdc)])
            ts("v", Wgk1b[:, kdc, :], Wgk1b[:, kdc, :], awc[:, kdc:kdc + 1], None,
               ALU.mult, ALU.bypass, [BWgk1b, Bawc], [BWgk1b])

        def d_stats(i):
            xt = stream[:, i, :]
            ln_stats_pair(i, 0, [(Bstream, i)], xt[:, 0:512], xt[:, 512:1024])
            c = 8 * i
            ts("v", stat[:, c + 5:c + 6], stat[:, c + 1:c + 2],
               stat[:, c + 1:c + 2], None, ALU.mult, ALU.bypass,
               [(Bstat, i)], [(Bstat, i)])
            ts("v", stat[:, c + 5:c + 6], stat[:, c + 5:c + 6], -1e-5, 1.0,
               ALU.mult, ALU.add, [(Bstat, i)], [(Bstat, i)])
            p.emit("s", lambda e, c=c: e.activation(
                out=stat[:, c:c + 1], in_=stat[:, c + 5:c + 6],
                func=AF.Ln, bias=RMS_EPS, scale=1.0),
                reads=[(Bstat, i), Bceps], writes=[(Bstat, i)])
            p.emit("s", lambda e, c=c: e.activation(
                out=stat[:, c + 3:c + 4], in_=stat[:, c:c + 1],
                func=AF.Exp, bias=0.0, scale=-0.5),
                reads=[(Bstat, i)], writes=[(Bstat, i)])
            p.emit("s", lambda e, c=c: e.activation(
                out=stat[:, c + 4:c + 5], in_=stat[:, c:c + 1],
                func=AF.Identity, bias=0.0, scale=-0.5),
                reads=[(Bstat, i)], writes=[(Bstat, i)])
            p.emit("s", lambda e, c=c: e.activation(
                out=stat[:, c + 5:c + 6], in_=stat[:, c:c + 1],
                func=AF.Exp, bias=0.0, scale=0.5),
                reads=[(Bstat, i)], writes=[(Bstat, i)])
            ts("v", stat[:, c:c + 1], stat[:, c + 3:c + 4], -1.0, None,
               ALU.mult, ALU.bypass, [(Bstat, i)], [(Bstat, i)])
            p.emit("t", lambda e, c=c: e.matmul(
                pU[0:1, 0:128], stat[:, c + 5:c + 6], c_id[:],
                start=True, stop=True, is_transpose=True),
                reads=[(Bstat, i), Bcid], writes=[(BpU, 0)])
            p.emit("v", lambda e, i=i: e.tensor_copy(
                rcrow[0:1, i, :], pU[0:1, 0:128]),
                reads=[(BpU, 0)], writes=[(Brcrow, i)])

        def d_h3T(i):
            norm_transpose(stream[:, i, :], i, 0, [(Bstream, i)], i % 2)

        def d_glog(i, want_expb):
            c = 8 * i
            tp = i % 2
            for kdc in range(8):
                p.emit("t", lambda e, kdc=kdc, tp=tp: e.matmul(
                    pX[:, 0:16], xhT[:, tp, kdc, :], Wgk1b[:, kdc, :],
                    start=(kdc == 0), stop=(kdc == 7)),
                    reads=[(BxhT, tp), BWgk1b],
                    writes=[(BpX, 0)] if kdc in (0, 7) else [])
            p.emit("s", lambda e: e.activation(
                out=z1s[:], in_=pX[:, 0:16], func=AF.Identity, bias=0.0, scale=1.0),
                reads=[(BpX, 0)], writes=[Bz1s])
            p.emit("t", lambda e: e.matmul(
                pTb[0:16, 0:128], z1s[:], c_idb[:],
                start=True, stop=True, is_transpose=True),
                reads=[Bz1s, Bcidb], writes=[(BpU, 0)])
            p.emit("v", lambda e: e.tensor_copy(z1T[:], pTb[0:16, 0:128]),
                   reads=[(BpU, 0)], writes=[Bz1T])
            p.emit("t", lambda e: e.matmul(
                pX[:, 0:512], z1T[:], Wgk2b[:], start=True, stop=False),
                reads=[Bz1T, BWgk2b], writes=[(BpX, 0)])
            p.emit("t", lambda e, i=i: e.matmul(
                pX[:, 0:512], rcrow[0:1, i, :], bgk2r[:], start=False, stop=True),
                reads=[(Brcrow, i), Bbgk2r], writes=[(BpX, 0)])
            p.emit("s", lambda e, c=c: e.activation(
                out=sigz[:], in_=pX[:, 0:512], func=AF.Exp,
                bias=0.0, scale=stat[:, c:c + 1]),
                reads=[(BpX, 0), (Bstat, i)], writes=[(Bsigz, 0), (Bsigz, 1)])
            p.emit("s", lambda e: e.activation(
                out=glcur[:], in_=sigz[:], func=AF.Ln, bias=1.0, scale=1.0),
                reads=[(Bsigz, 0), (Bsigz, 1)], writes=[Bglcur])
            p.emit("t", lambda e: e.matmul(
                pX[:, 512:1024], c_U2b[:], glcur[:], start=True, stop=True),
                reads=[BcU2b, Bglcur], writes=[(BpX, 1)])
            sgn = 1.0 if want_expb else -1.0
            p.emit("s", lambda e, c=c, sgn=sgn: e.activation(
                out=expsc[:], in_=pX[:, 512:1024], func=AF.Exp,
                bias=stat[:, c + 4:c + 5], scale=sgn),
                reads=[(BpX, 1), (Bstat, i)], writes=[Bexpsc])

        def d_eblast(i):
            for ci, (cind, bci) in enumerate(((c_ind0, Bci0), (c_ind1, Bci1))):
                p.emit("t", lambda e, cind=cind, ci=ci, bci=bci: e.matmul(
                    pX[0:1, 512 * ci:512 * (ci + 1)], cind[:], glcur[:],
                    start=True, stop=True),
                    reads=[bci, Bglcur], writes=[(BpX, ci)])
            for ci in range(2):
                p.emit("s", lambda e, ci=ci: e.activation(
                    out=ebrow[:], in_=pX[0:1, 512 * ci:512 * (ci + 1)],
                    func=AF.Exp, bias=0.0, scale=1.0),
                    reads=[(BpX, ci)], writes=[Bebrow])
                for j2 in range(4):
                    p.emit("t", lambda e, j2=j2: e.matmul(
                        pU[:, 0:1], ebrow[0:1, 128 * j2:128 * (j2 + 1)],
                        c_id[0:1, 0:1], start=True, stop=True, is_transpose=True),
                        reads=[Bebrow, Bcid], writes=[(BpU, 0)])
                    p.emit("v", lambda e, j2=j2, ci=ci, i=i: e.tensor_copy(
                        ebc[:, i, 4 * ci + j2:4 * ci + j2 + 1], pU[:, 0:1]),
                        reads=[(BpU, 0)], writes=[(Bebc, i)])

        def p1_chain(i):
            for ch2 in range(2):
                rs = slice(64 * ch2, 64 * (ch2 + 1))
                for h in range(NH):
                    hp = h % 2
                    usl = slice(256 + 512 * hp, 512 + 512 * hp)
                    tsl = slice(256 * hp, 256 * (hp + 1))
                    p.emit("t", lambda e, rs=rs, h=h, i=i, usl=usl: e.matmul(
                        pU[:, usl], kexpS[rs, i, 128 * h:128 * (h + 1)],
                        vS[rs, i, 256 * h:256 * (h + 1)], start=True, stop=True),
                        reads=[(BkexpS, i), (BvS, i)], writes=[(BpUu, hp)])
                    tt("v", tscrD[:, tsl], Sgla[:, h, :], pU[:, usl],
                       ALU.add, [(BSgla, h), (BpUu, hp)], [(BtscrD, hp)])
                    p.emit("s", lambda e, h=h, i=i, ch2=ch2, tsl=tsl:
                           e.activation(
                               out=Sgla[:, h, :], in_=tscrD[:, tsl],
                               func=AF.Identity, bias=0.0,
                               scale=ebc[:, i, 4 * ch2 + h:4 * ch2 + h + 1]),
                           reads=[(BtscrD, hp), (Bebc, i)], writes=[(BSgla, h)])

        for h in range(NH):
            p.emit("g", lambda e, h=h: e.memset(Sgla[:, h, :], 0.0),
                   writes=[(BSgla, h)])
        d_stats(0)
        d_h3T(0)
        for i in range(NT):
            d_glog(i, want_expb=False)
            d_eblast(i)
            if i + 1 < NT:
                d_stats(i + 1)
            for kdc in range(8):
                p.emit("t", lambda e, kdc=kdc, i=i: e.matmul(
                    pX[:, 1024:1536], xhT[:, i % 2, kdc, :], WkB[:, kdc, :],
                    start=(kdc == 0), stop=(kdc == 7)),
                    reads=[(BxhT, i % 2), (BWkB, kdc)],
                    writes=[(BpX, 2)] if kdc in (0, 7) else [])
            p.emit("v", lambda e, i=i: e.tensor_tensor(
                out=kexpS[:, i, :], in0=pX[:, 1024:1536], in1=expsc[:],
                op=ALU.mult),
                reads=[(BpX, 2), Bexpsc], writes=[(BkexpS, i)])
            if i + 1 < NT:
                d_h3T(i + 1)
            c = 8 * i
            for nh2 in range(2):
                sl = slice(512 * nh2, 512 * (nh2 + 1))
                for kdc in range(8):
                    p.emit("t", lambda e, kdc=kdc, sl=sl, nh2=nh2, i=i: e.matmul(
                        pY[:, sl], xhT[:, i % 2, kdc, :], WvB[:, kdc, sl],
                        start=(kdc == 0), stop=(kdc == 7)),
                        reads=[(BxhT, i % 2), (BWvB, kdc)],
                        writes=[(BpY, nh2)] if kdc in (0, 7) else [])
                p.emit("s", lambda e, c=c, sl=sl, i=i, nh2=nh2: e.activation(
                    out=vS[:, i, sl], in_=pY[:, sl], func=AF.Identity,
                    bias=0.0, scale=stat[:, c + 3:c + 4]),
                    reads=[(BpY, nh2), (Bstat, i)], writes=[(BvS, i)])
            if i > 0:
                p1_chain(i - 1)
        p1_chain(NT - 1)

        for h in range(NH):
            dma("g", ag2_in[:, 256 * h:256 * (h + 1)], Sgla[:, h, :],
                reads=[(BSgla, h)])
        cc2 = p.emit("g", lambda e: e.collective_compute(
            "AllGather", ALU.bypass, ins=[ag2_in.opt()], outs=[ag2_out.opt()],
            replica_groups=[list(range(NCORE))]),
            extra_waits=[("dg" + str(i), p.counts["dg" + str(i)])
                         for i in range(4)],
            is_collective=True)
        for h in range(NH):
            p.emit("g", lambda e, h=h: e.memset(Sgla[:, h, :], 0.0),
                   writes=[(BSgla, h)])
        for r in range(NCORE):
            p.emit("g", lambda e, r=r: e.dma_start(
                out=agh2[:], in_=ag2_out[128 * r:128 * (r + 1), :]),
                writes=[Bagh2], extra_waits=[cc2], dma=True)
            for h in range(NH):
                stt("v", Sgla[:, h, :], agh2[:, 256 * h:256 * (h + 1)],
                    selcs[:, r:r + 1], Sgla[:, h, :], ALU.mult, ALU.add,
                    [Bagh2, Bsel, (BSgla, h)], [(BSgla, h)])
        for h in range(NH):
            p.emit("v", lambda e, h=h: e.tensor_copy(
                Sb16[:, h, :], Sgla[:, h, :]),
                reads=[(BSgla, h)], writes=[(BSb16, h)])

        spillT = spill.rearrange("(tt p) d -> p tt d", p=128)

        sbfree("WkB", "WvB")
        WgB, BWgB = sb("WgB", (128, 8, D), BF16, nslot=8)
        WoB, BWoB = sb("WoB", (128, 8, D), BF16, nslot=8)
        qf, Bqf = sb("qf", (128, NH, 128), BF16)
        qeT0, BqeT0 = sb("qeT0", (128, 2, 128), BF16, nslot=2)
        qeT1, BqeT1 = sb("qeT1", (128, 2, 128), BF16, nslot=2)
        kexpT, BkexpT = sb("kexpT", (128, 2, 128), BF16, nslot=2)
        Atm, BAtm = sb("Atm", (128, 2, 128), BF16, nslot=2)
        silug, Bsilug = sb("silug", (128, D), F16)
        obuf, Bobuf = sb("obuf", (128, NH, 256), BF16, nslot=NH)
        oT, BoT = sb("oT", (128, 8, 128), BF16)
        gns, Bgns = sb("gns", (128, 2, 8), F32, nslot=2)
        o2s, Bo2s = sb("o2s", (128, 2, 256), BF16, nslot=2)
        ob32, Bob32 = sb("ob32", (128, 2, 256), F32, nslot=2)
        x2bt, Bx2bt = agh2, Bagh2
        WgBr = dr["Wg"].rearrange("(c p) o -> p c o", p=128)
        for kdc in range(8):
            dma("g", WgB[:, kdc, :], WgBr[:, kdc, :], writes=[(BWgB, kdc)])
        WoBr = dr["Wo"].rearrange("(c p) o -> p c o", p=128)
        for kdc in range(8):
            dma("g", WoB[:, kdc, :], WoBr[:, kdc, :], writes=[(BWoB, kdc)])
        for kdc in range(8):
            ts("v", WgB[:, kdc, :], WgB[:, kdc, :], awc[:, kdc:kdc + 1], None,
               ALU.mult, ALU.bypass, [(BWgB, kdc), Bawc], [(BWgB, kdc)])
            ts("v", WoB[:, kdc, :], WoB[:, kdc, :], gnwc[:, kdc % 2:kdc % 2 + 1],
               None, ALU.mult, ALU.bypass, [(BWoB, kdc), Bgnwc], [(BWoB, kdc)])

        for s2 in range(2):
            p.emit("g", lambda e, s2=s2: e.memset(qeT0[:, s2, 64:128], 0.0),
                   writes=[(BqeT0, s2)])
            p.emit("g", lambda e, s2=s2: e.memset(qeT1[:, s2, 0:64], 0.0),
                   writes=[(BqeT1, s2)])

        def d_front(i):
            c = 8 * i
            d_h3T(i)
            d_glog(i, want_expb=True)
            tp = i % 2
            for h in range(NH):
                for kdc in range(8):
                    p.emit("t", lambda e, kdc=kdc, h=h, tp=tp: e.matmul(
                        pX[:, 1024 + 128 * h:1152 + 128 * h],
                        WqB[:, kdc, 128 * h:128 * (h + 1)], xhT[:, tp, kdc, :],
                        start=(kdc == 0), stop=(kdc == 7)),
                        reads=[(BWqB, kdc), (BxhT, tp)],
                        writes=[(BpX, 2)] if kdc in (0, 7) else [])
                p.emit("s", lambda e, h=h: e.activation(
                    out=qf[:, h, :], in_=pX[:, 1024 + 128 * h:1152 + 128 * h],
                    func=AF.Identity, bias=0.0, scale=1.0),
                    reads=[(BpX, 2)], writes=[Bqf])
            for nh2 in range(2):
                sl = slice(512 * nh2, 512 * (nh2 + 1))
                for kdc in range(8):
                    p.emit("t", lambda e, kdc=kdc, sl=sl, tp=tp: e.matmul(
                        pX[:, sl], xhT[:, tp, kdc, :], WgB[:, kdc, sl],
                        start=(kdc == 0), stop=(kdc == 7)),
                        reads=[(BxhT, tp), (BWgB, kdc)],
                        writes=[(BpX, nh2)] if kdc in (0, 7) else [])
                p.emit("s", lambda e, c=c, sl=sl, nh2=nh2: e.activation(
                    out=silug[:, sl], in_=pX[:, sl], func=AF.Silu,
                    bias=0.0, scale=stat[:, c + 3:c + 4]),
                    reads=[(BpX, nh2), (Bstat, i)], writes=[Bsilug])

        def d_head_front(i, h):
            hp = h % 2
            osl = slice(512 * hp, 512 * hp + 256)
            p.emit("t", lambda e, h=h: e.matmul(
                pU[:, 0:128], expsc[:, 128 * h:128 * (h + 1)], c_id[:],
                start=True, stop=True, is_transpose=True),
                reads=[Bexpsc, Bcid], writes=[(BpU, 0)])
            p.emit("v", lambda e, h=h, hp=hp: e.scalar_tensor_tensor(
                out=qeT0[:, hp, 0:64], in0=pU[:, 0:64], scalar=QESC,
                in1=qf[:, h, 0:64], op0=ALU.mult, op1=ALU.mult),
                reads=[(BpU, 0), Bqf], writes=[(BqeT0, hp)])
            p.emit("v", lambda e, h=h, hp=hp: e.scalar_tensor_tensor(
                out=qeT1[:, hp, 64:128], in0=pU[:, 64:128], scalar=QESC,
                in1=qf[:, h, 64:128], op0=ALU.mult, op1=ALU.mult),
                reads=[(BpU, 0), Bqf], writes=[(BqeT1, hp)])
            p.emit("t", lambda e, h=h, i=i: e.matmul(
                pTb[:, 1024:1152], kexpS[:, i, 128 * h:128 * (h + 1)],
                c_idb[:], start=True, stop=True, is_transpose=True),
                reads=[(BkexpS, i), Bcidb], writes=[(BpU, 1)])
            p.emit("v", lambda e, hp=hp: e.tensor_copy(
                kexpT[:, hp, :], pTb[:, 1024:1152]),
                reads=[(BpU, 1)], writes=[(BkexpT, hp)])
            asl = slice(1536 + 128 * hp, 1664 + 128 * hp)
            p.emit("t", lambda e, hp=hp, asl=asl: e.matmul(
                pX[:, asl], kexpT[:, hp, :], qeT0[:, hp, :],
                start=True, stop=False),
                reads=[(BqeT0, hp), (BkexpT, hp)], writes=[(BpX, 3)])
            p.emit("t", lambda e, hp=hp, asl=asl: e.matmul(
                pX[:, asl], kexpT[:, hp, :], qeT1[:, hp, :],
                start=False, stop=True),
                reads=[(BkexpT, hp), (BqeT1, hp)], writes=[(BpX, 3)])
            p.emit("v", lambda e, hp=hp, asl=asl: e.tensor_tensor(
                out=Atm[:, hp, :], in0=pX[:, asl], in1=c_mask2[:], op=ALU.mult),
                reads=[(BpX, 3), Bcm2], writes=[(BAtm, hp)])
            p.emit("t", lambda e, h=h, i=i: e.matmul(
                pU[:, 256:512], kexpS[0:64, i, 128 * h:128 * (h + 1)],
                vS[0:64, i, 256 * h:256 * (h + 1)], start=True, stop=True),
                reads=[(BkexpS, i), (BvS, i)], writes=[(BpUu, 0)])
            p.emit("t", lambda e, h=h, i=i: e.matmul(
                pU[:, 768:1024], kexpS[64:128, i, 128 * h:128 * (h + 1)],
                vS[64:128, i, 256 * h:256 * (h + 1)], start=True, stop=True),
                reads=[(BkexpS, i), (BvS, i)], writes=[(BpUu, 1)])
            p.emit("t", lambda e, h=h, i=i, hp=hp, osl=osl: e.matmul(
                pY[:, osl], Atm[:, hp, :], vS[:, i, 256 * h:256 * (h + 1)],
                start=True, stop=False),
                reads=[(BAtm, hp), (BvS, i)], writes=[(BpY, hp)])
            p.emit("t", lambda e, h=h, hp=hp, osl=osl: e.matmul(
                pY[:, osl], qeT0[:, hp, :], Sb16[:, h, :],
                start=False, stop=True),
                reads=[(BqeT0, hp), (BSb16, h)], writes=[(BpY, hp)])
            tsl = slice(256 * hp, 256 * (hp + 1))
            tt("v", tscrD[:, tsl], Sgla[:, h, :], pU[:, 256:512],
               ALU.add, [(BSgla, h), (BpUu, 0)], [(BtscrD, hp)])
            p.emit("s", lambda e, h=h, i=i, tsl=tsl: e.activation(
                out=Sgla[:, h, :], in_=tscrD[:, tsl], func=AF.Identity,
                bias=0.0, scale=ebc[:, i, h:h + 1]),
                reads=[(BtscrD, hp), (Bebc, i)], writes=[(BSgla, h)])
            p.emit("s", lambda e, h=h, hp=hp: e.activation(
                out=Sb16[:, NH + hp, :], in_=Sgla[:, h, :], func=AF.Identity,
                bias=0.0, scale=1.0),
                reads=[(BSgla, h)], writes=[(BSb16, NH + hp)])
            tt("v", tscrD[:, tsl], Sgla[:, h, :], pU[:, 768:1024],
               ALU.add, [(BSgla, h), (BpUu, 1)], [(BtscrD, hp)])
            p.emit("s", lambda e, h=h, i=i, tsl=tsl: e.activation(
                out=Sgla[:, h, :], in_=tscrD[:, tsl], func=AF.Identity,
                bias=0.0, scale=ebc[:, i, 4 + h:4 + h + 1]),
                reads=[(BtscrD, hp), (Bebc, i)], writes=[(BSgla, h)])
            p.emit("s", lambda e, h=h: e.activation(
                out=Sb16[:, h, :], in_=Sgla[:, h, :], func=AF.Identity,
                bias=0.0, scale=1.0),
                reads=[(BSgla, h)], writes=[(BSb16, h)])

        def d_head_close(i, h):
            hp = h % 2
            osl = slice(512 * hp, 512 * hp + 256)
            o2l = slice(512 * hp + 256, 512 * hp + 512)
            p.emit("t", lambda e, h=h, hp=hp, o2l=o2l: e.matmul(
                pY[:, o2l], qeT1[:, hp, :], Sb16[:, NH + hp, :],
                start=True, stop=True),
                reads=[(BqeT1, hp), (BSb16, NH + hp)], writes=[(BpY, hp)])
            p.emit("s", lambda e, hp=hp, o2l=o2l: e.activation(
                out=o2s[:, hp, :], in_=pY[:, o2l], func=AF.Identity,
                bias=0.0, scale=1.0),
                reads=[(BpY, hp)], writes=[(Bo2s, hp)])
            tt("v", ob32[:, hp, :], o2s[:, hp, :], pY[:, osl], ALU.add,
               [(Bo2s, hp), (BpY, hp)], [(Bob32, hp)])
            p.emit("v", lambda e, hp=hp: e.bn_stats(
                gns[:, hp, 0:6], ob32[:, hp, :]),
                reads=[(Bob32, hp)], writes=[(Bgns, hp)])
            p.emit("v", lambda e, hp=hp: e.bn_aggr(
                gns[:, hp, 6:8], gns[:, hp, 0:6]),
                reads=[(Bgns, hp)], writes=[(Bgns, hp)])
            p.emit("v", lambda e, hp=hp: e.tensor_scalar(
                out=gns[:, hp, 0:1], in0=gns[:, hp, 6:7],
                scalar1=gns[:, hp, 6:7], scalar2=gns[:, hp, 7:8],
                op0=ALU.mult, op1=ALU.add),
                reads=[(Bgns, hp)], writes=[(Bgns, hp)])
            p.emit("s", lambda e, hp=hp: e.activation(
                out=gns[:, hp, 1:2], in_=gns[:, hp, 0:1], func=AF.Ln,
                bias=RMS_EPS, scale=1.0),
                reads=[(Bgns, hp), Bceps], writes=[(Bgns, hp)])
            p.emit("s", lambda e, hp=hp: e.activation(
                out=gns[:, hp, 2:3], in_=gns[:, hp, 1:2], func=AF.Exp,
                bias=0.0, scale=-0.5),
                reads=[(Bgns, hp)], writes=[(Bgns, hp)])
            p.emit("v", lambda e, h=h, hp=hp:
                   e.scalar_tensor_tensor(
                       out=obuf[:, h, :], in0=ob32[:, hp, :],
                       scalar=gns[:, hp, 2:3],
                       in1=silug[:, 256 * h:256 * (h + 1)],
                       op0=ALU.mult, op1=ALU.mult),
                   reads=[(Bob32, hp), (Bgns, hp), Bsilug],
                   writes=[(Bobuf, h)])

        d_front(0)
        _DEFER = False
        for i in range(NT):
            c = 8 * i
            for h in range(NH):
                d_head_front(i, h)
                if not _DEFER:
                    d_head_close(i, h)
                elif h > 0:
                    d_head_close(i, h - 1)
            if _DEFER:
                d_head_close(i, NH - 1)
            if i + 1 < NT:
                d_front(i + 1)
            oflat = obuf[:].rearrange("p h e -> p (h e)")
            for kdc in range(8):
                tp128(oflat[:, 128 * kdc:128 * (kdc + 1)], [(Bobuf, kdc // 2)],
                      oT[:, kdc, :], [BoT], kdc)
            for nh2 in range(2):
                sl = slice(512 * nh2, 512 * (nh2 + 1))
                for kdc in range(8):
                    p.emit("t", lambda e, kdc=kdc, sl=sl, nh2=nh2: e.matmul(
                        pY[:, sl], oT[:, kdc, :], WoB[:, kdc, sl],
                        start=(kdc == 0), stop=(kdc == 7)),
                        reads=[BoT, (BWoB, kdc)],
                        writes=[(BpY, nh2)] if kdc in (0, 7) else [])
            stt("v", x2bt[:], stream[:, i, :], stat[:, c + 1:c + 2],
                pY[:, 0:1024], ALU.mult, ALU.add,
                [(Bstream, i), (Bstat, i), (BpY, 0), (BpY, 1)], [Bx2bt])
            ts("v", x2bt[:], x2bt[:], stat[:, c + 2:c + 3], None,
               ALU.add, ALU.bypass, [Bx2bt, (Bstat, i)], [Bx2bt])
            sl2 = i % 2
            p.emit("v", lambda e, sl2=sl2: e.bn_stats(
                bnscr[:, sl2, 0:6], x2bt[:, 0:512]),
                reads=[Bx2bt], writes=[(Bbn, sl2)])
            p.emit("v", lambda e, sl2=sl2: e.bn_stats(
                bnscr[:, sl2, 6:12], x2bt[:, 512:1024]),
                reads=[Bx2bt], writes=[(Bbn, sl2)])
            p.emit("v", lambda e, sl2=sl2: e.bn_aggr(
                mvt[:, sl2, :], bnscr[:, sl2, :]),
                reads=[(Bbn, sl2)], writes=[(Bmvt, sl2)])
            p.emit("v", lambda e, c=c, sl2=sl2: e.tensor_scalar(
                out=stat[:, c + 6:c + 7], in0=mvt[:, sl2, 0:1],
                scalar1=mvt[:, sl2, 0:1], scalar2=mvt[:, sl2, 1:2],
                op0=ALU.mult, op1=ALU.add),
                reads=[(Bmvt, sl2)], writes=[(Bstat, i)])
            p.emit("s", lambda e, c=c: e.activation(
                out=stat[:, c + 6:c + 7], in_=stat[:, c + 6:c + 7],
                func=AF.Ln, bias=RMS_EPS, scale=1.0),
                reads=[(Bstat, i), Bceps], writes=[(Bstat, i)])
            p.emit("s", lambda e, c=c: e.activation(
                out=stat[:, c + 7:c + 8], in_=stat[:, c + 6:c + 7],
                func=AF.Exp, bias=0.0, scale=-0.5),
                reads=[(Bstat, i)], writes=[(Bstat, i)])
            slot = i % 2
            p.emit("s", lambda e, c=c, slot=slot: e.activation(
                out=xh[:, slot, :], in_=x2bt[:], func=AF.Identity,
                bias=0.0, scale=stat[:, c + 7:c + 8]),
                reads=[Bx2bt, (Bstat, i)], writes=[(Bxh, slot)])
            dma("y", spillT[:, i, :], xh[:, slot, :], reads=[(Bxh, slot)])
            p.emit("g", lambda e, i=i: e.tensor_tensor(
                out=stream[:, i, :], in0=stream[:, i, :], in1=x2bt[:],
                op=ALU.add),
                reads=[Bx2bt, (Bstream, i)], writes=[(Bstream, i)])

        sbfree("gnwc", "gnwcrow", "WqB", "WgB", "WoB", "qf", "qeT0", "qeT1", "kexpT", "Atm",
               "silug", "obuf", "oT", "gns", "o2s", "ob32",
               "kexpS", "vS", "ebc", "Sgla", "Sb16", "rcrow", "agh2",
               "sigz", "glcur", "expsc", "z1s", "z1T",
               "ebrow", "gcol", "Wgk1b", "Wgk2b", "bgk2r", "awc", "awcrow",
               "c_U2b", "c_mask2", "c_ind0", "c_ind1")

        w2e, Bw2e = sb("w2qe", (128, 32, D), FP8, nslot=4)
        mwc, Bmwc = sb("mwc", (128, 8), F32)
        mwcrow, Bmwcrow = sb("mwcrow", (8, 128), F32)
        dma("s", mwcrow[:], dr["mlp_w"].rearrange("(c p) -> c p", p=128),
            writes=[Bmwcrow])
        p.emit("t", lambda e: e.matmul(
            pU[:, 0:8], mwcrow[:], c_id[0:8, 0:8],
            start=True, stop=True, is_transpose=True),
            reads=[Bmwcrow, Bcid], writes=[(BpU, 0)])
        p.emit("v", lambda e: e.tensor_copy(mwc[:], pU[:, 0:8]),
               reads=[(BpU, 0)], writes=[Bmwc])
        MIH = MI // 2
        WgaB, BWgaB = sb("WgaB", (128, 8, MIH), BF16, nslot=8)
        WupB, BWupB = sb("WupB", (128, 8, MIH), BF16, nslot=8)
        WdoB, BWdoB = sb("WdoB", (128, 11, D), BF16, nslot=11)
        sgb2, Bsgb2 = sb("sgb2", (128, 2, 512), F16, nslot=2)
        mtl2, Bmtl2 = sb("mtl2", (128, 2, 512), BF16, nslot=2)
        mtT2, BmtT2 = sb("mtT2", (128, 2, 4, 128), BF16, nslot=2)

        def mlp_reload(i):
            sl2 = i % 2
            dma("y", xh[:, sl2, :], spillT[:, i, :], writes=[(Bxh, sl2)])

        def mlp_tp(i):
            sl2 = i % 2
            for kdc in range(8):
                tp128(xh[:, sl2, 128 * kdc:128 * (kdc + 1)], [(Bxh, sl2)],
                      xhT[:, sl2, kdc, :], [(BxhT, sl2)], kdc)

        for fh in range(2):
            fsl = slice(MIH * fh, MIH * (fh + 1))
            Wgar = dr["Wgate"][:, fsl].rearrange("(c p) o -> p c o", p=128)
            Wupr = dr["Wup"][:, fsl].rearrange("(c p) o -> p c o", p=128)
            Wdor = dr["Wdown"][fsl, :].rearrange("(c p) o -> p c o", p=128)
            for kdc in range(8):
                dma("g", WgaB[:, kdc, :], Wgar[:, kdc, :], writes=[(BWgaB, kdc)])
                dma("g", WupB[:, kdc, :], Wupr[:, kdc, :], writes=[(BWupB, kdc)])
            for fc in range(11):
                dma("g", WdoB[:, fc, :], Wdor[:, fc, :], writes=[(BWdoB, fc)])
            if fh == 0:
                w2re = dr["w2T"].rearrange("(c p) d -> p c d", p=128)
                for jq in range(4):
                    dma("g", w2e[:, 8 * jq:8 * (jq + 1), :],
                        w2re[:, 8 * jq:8 * (jq + 1), :], writes=[(Bw2e, jq)])
            for kdc in range(8):
                ts("v", WgaB[:, kdc, :], WgaB[:, kdc, :], mwc[:, kdc:kdc + 1],
                   None, ALU.mult, ALU.bypass, [(BWgaB, kdc), Bmwc],
                   [(BWgaB, kdc)])
                ts("v", WupB[:, kdc, :], WupB[:, kdc, :], mwc[:, kdc:kdc + 1],
                   None, ALU.mult, ALU.bypass, [(BWupB, kdc), Bmwc],
                   [(BWupB, kdc)])
            NBK = [512, 512, 384]
            mlp_reload(0)
            mlp_tp(0)
            for i in range(NT):
                sl2 = i % 2
                off = 0
                for nb, w in enumerate(NBK):
                    np_ = nb % 2
                    slq = slice(off, off + w)
                    gsl = slice(512 * np_, 512 * np_ + w)
                    usl_ = slice(1024 + 512 * np_, 1024 + 512 * np_ + w)
                    for kdc in range(8):
                        p.emit("t", lambda e, kdc=kdc, slq=slq, gsl=gsl,
                               sl2=sl2: e.matmul(
                                   pX[:, gsl], xhT[:, sl2, kdc, :],
                                   WgaB[:, kdc, slq],
                                   start=(kdc == 0), stop=(kdc == 7)),
                               reads=[(BxhT, sl2), (BWgaB, kdc)],
                               writes=pb_(BpX, gsl) if kdc in (0, 7) else [])
                    if nb == 0 and i + 1 < NT:
                        mlp_reload(i + 1)
                    for kdc in range(8):
                        p.emit("t", lambda e, kdc=kdc, slq=slq, usl_=usl_,
                               sl2=sl2: e.matmul(
                                   pX[:, usl_], xhT[:, sl2, kdc, :],
                                   WupB[:, kdc, slq],
                                   start=(kdc == 0), stop=(kdc == 7)),
                               reads=[(BxhT, sl2), (BWupB, kdc)],
                               writes=pb_(BpX, usl_) if kdc in (0, 7) else [])
                    p.emit("s", lambda e, w=w, np_=np_, gsl=gsl: e.activation(
                        out=sgb2[:, np_, 0:w], in_=pX[:, gsl], func=AF.Silu,
                        bias=0.0, scale=1.0),
                        reads=pb_(BpX, gsl), writes=[(Bsgb2, np_)])
                    p.emit("v", lambda e, w=w, np_=np_, usl_=usl_:
                           e.tensor_tensor(
                               out=mtl2[:, np_, 0:w], in0=pX[:, usl_],
                               in1=sgb2[:, np_, 0:w], op=ALU.mult),
                           reads=pb_(BpX, usl_) + [(Bsgb2, np_)],
                           writes=[(Bmtl2, np_)])
                    if nb == 1 and i + 1 < NT:
                        mlp_tp(i + 1)
                    nch = w // 128
                    for fc in range(nch):
                        tp128(mtl2[:, np_, 128 * fc:128 * (fc + 1)],
                              [(Bmtl2, np_)],
                              mtT2[:, np_, fc, :], [(BmtT2, np_)], fc)
                    for nh2 in range(2):
                        sl = slice(512 * nh2, 512 * (nh2 + 1))
                        for fc in range(nch):
                            gfc = off // 128 + fc
                            p.emit("t", lambda e, fc=fc, gfc=gfc, sl=sl, nb=nb,
                                   np_=np_:
                                   e.matmul(pY[:, sl], mtT2[:, np_, fc, :],
                                            WdoB[:, gfc, sl],
                                            start=(nb == 0 and fc == 0),
                                            stop=(nb == len(NBK) - 1 and
                                                  fc == nch - 1)),
                                   reads=[(BmtT2, np_), (BWdoB, gfc)],
                                   writes=[(BpY, nh2)]
                                   if (nb == 0 and fc == 0) or
                                      (nb == len(NBK) - 1 and fc == nch - 1)
                                   else [])
                    off += w
                p.emit("v", lambda e, i=i: e.tensor_tensor(
                    out=stream[:, i, :], in0=stream[:, i, :], in1=pY[:, 0:1024],
                    op=ALU.add),
                    reads=[(BpY, 0), (BpY, 1), (Bstream, i)],
                    writes=[(Bstream, i)])
        sbfree("WgaB", "WupB", "WdoB", "sgb2", "mtl2", "mtT2", "mwc", "mwcrow")

    if n_stage >= 4:
        ffn_stage("e", pre_w2=(w2e, Bw2e))

    for i in range(NT):
        dma("y", outT[:, i, :], stream[:, i, :], reads=[(Bstream, i)])
    p.wait_only("y", extra_waits=[("dy" + str(i), p.counts["dy" + str(i)])
                                  for i in range(4)])
    p.replay()
    return nc


def kernel(**inputs):
    n_stage = int(inputs.pop("_n_stage", N_STAGE))
    key = ("nc", n_stage)
    if key not in _CACHE:
        _CACHE[key] = build_nc(n_stage)
    nc = _CACHE[key]
    consts = _host_consts()

    x = np.ascontiguousarray(np.asarray(inputs["L0_attn_a_x"], dtype=np.float32))
    base = {
        "W_inT": inputs["L0_attn_a_W_in"].T,
        "b_in": inputs["L0_attn_a_b_in"],
        "gamma": inputs["L0_attn_a_gamma"], "beta": inputs["L0_attn_a_beta"],
        "A_re": np.concatenate([inputs["L0_attn_a_A_re"], [1.0, 0.0]]),
        "A_im": np.concatenate([inputs["L0_attn_a_A_im"], [0.0, 0.0]]),
        "B_re": np.concatenate([inputs["L0_attn_a_B_re"], [1.0, 0.0]]),
        "B_im": np.concatenate([inputs["L0_attn_a_B_im"], [0.0, 0.0]]),
        "CreT": np.concatenate([inputs["L0_attn_a_C_re"].T,
                                np.asarray(inputs["L0_attn_a_D_re"],
                                           dtype=np.float32)[None, :],
                                np.zeros((1, D), dtype=np.float32)], axis=0),
        "CimT": np.concatenate([inputs["L0_attn_a_C_im"].T,
                                np.asarray(inputs["L0_attn_a_D_im"],
                                           dtype=np.float32)[None, :],
                                np.zeros((1, D), dtype=np.float32)], axis=0),
        "w1T": inputs["L0_ffn_w1"].T, "b1": inputs["L0_ffn_b1"],
        "w2T": inputs["L0_ffn_w2"].T, "b2": inputs["L0_ffn_b2"],
        "attn_w": inputs["L0_attn_b_attn_norm_w"],
        "Wq": inputs["L0_attn_b_Wq"], "Wk": inputs["L0_attn_b_Wk"],
        "Wv": inputs["L0_attn_b_Wv"], "Wg": inputs["L0_attn_b_Wg"],
        "Wgk1": inputs["L0_attn_b_Wgk1"], "Wgk2": inputs["L0_attn_b_Wgk2"],
        "bgk2": inputs["L0_attn_b_bgk2"], "gnw": inputs["L0_attn_b_gnorm_w"],
        "Wo": inputs["L0_attn_b_Wo"], "mlp_w": inputs["L0_attn_b_mlp_norm_w"],
        "Wgate": inputs["L0_attn_b_W_gate"], "Wup": inputs["L0_attn_b_W_up"],
        "Wdown": inputs["L0_attn_b_W_down"],
    }
    base = {k: np.ascontiguousarray(np.asarray(v, dtype=np.float32))
            for k, v in base.items()}
    base.update(consts)

    in_maps = []
    for c in range(NCORE):
        b, h = c // 2, c % 2
        m = dict(base)
        m["x0"] = np.ascontiguousarray(x[b, h * TC:(h + 1) * TC, :])
        sel = np.zeros((128, 8), dtype=np.float32)
        if h == 1:
            sel[:, 2 * b] = 1.0
        m["selc"] = sel
        in_maps.append(m)

    res = run_bass_kernel_spmd(nc, in_maps, core_ids=list(range(NCORE)))
    full = np.empty((B, T, D), dtype=np.float32)
    for c in range(NCORE):
        b, h = c // 2, c % 2
        full[b, h * TC:(h + 1) * TC, :] = res.results[c]["out"]
    return full
